# version 18
# baseline (speedup 1.0000x reference)
import math
import numpy as np
import ml_dtypes
import concourse.bass as bass
import concourse.mybir as mybir
from concourse.bass_utils import run_bass_kernel_spmd
from concourse.alu_op_type import AluOpType as OP

F32 = mybir.dt.float32
BF16 = mybir.dt.bfloat16
AF = mybir.ActivationFunctionType
AX = mybir.AxisListType
bfloat16 = ml_dtypes.bfloat16

C = 2048
TC = 2048
TT = 1024
L = 64
NCH = 16
NSLAB = 8
LN8 = math.log(1.0 / 8.0)

DEBUG = []
LIMIT = 10**9
S1N = 8
S1CUT = 9
S5CUT = 9


def _prep_proj(M):
    return np.ascontiguousarray(
        M.reshape(16, 128, 16, 128).transpose(2, 1, 0, 3).reshape(16, 128, 2048))


def _colfm(v):
    return np.ascontiguousarray(v.reshape(16, 128).T)


def build_inputs(ins):
    x = np.asarray(ins["L0_attn_x"], dtype=np.float32)
    g = {}
    g["Wq"] = _prep_proj(np.asarray(ins["L0_attn_q_w"], np.float32))
    g["Wk"] = _prep_proj(np.asarray(ins["L0_attn_k_w"], np.float32))
    g["Wv"] = _prep_proj(np.asarray(ins["L0_attn_v_w"], np.float32))
    g["Wo"] = _prep_proj(np.asarray(ins["L0_attn_o_w"], np.float32))
    g["Wf"] = _prep_proj(np.ascontiguousarray(
        np.asarray(ins["L0_ffn_W"], np.float32).T))
    A = np.asarray(ins["L0_ffn_A"], np.float32)
    Wa = np.ascontiguousarray(A.transpose(1, 0, 2).reshape(256, 2048).T)
    g["Wa"] = np.ascontiguousarray(
        Wa.reshape(16, 128, 2, 128).transpose(2, 1, 0, 3).reshape(2, 128, 2048))
    Bm = np.asarray(ins["L0_ffn_B"], np.float32)
    Wb = np.ascontiguousarray(Bm.transpose(2, 0, 1).reshape(256, 2048))
    g["Wb"] = np.ascontiguousarray(
        Wb.reshape(2, 128, 16, 128).transpose(2, 1, 0, 3).reshape(16, 128, 256))
    g["Wmm"] = np.ascontiguousarray(
        np.asarray(ins["L0_attn_maa_w1"], np.float32)
        .reshape(16, 128, 128).transpose(1, 0, 2).reshape(128, 2048))
    g["Wm2"] = np.ascontiguousarray(
        np.asarray(ins["L0_attn_maa_w2"], np.float32).reshape(128, 16, 128))
    g["Wt1"] = np.ascontiguousarray(
        np.asarray(ins["L0_attn_td_w1"], np.float32)
        .reshape(16, 128, 64).transpose(1, 0, 2).reshape(128, 1024))
    g["Wt2"] = np.ascontiguousarray(np.asarray(ins["L0_attn_td_w2"], np.float32))
    g["Wr1"] = np.ascontiguousarray(
        np.asarray(ins["L0_ffn_rw1"], np.float32).T
        .reshape(16, 128, 64).transpose(1, 0, 2).reshape(128, 1024))
    g["Wr2"] = np.ascontiguousarray(np.asarray(ins["L0_ffn_rw2"], np.float32)[:16].T)
    for nm, key in [("maax_c", "L0_attn_maa_x"), ("maar_c", "L0_attn_maa_r"),
                    ("maak_c", "L0_attn_maa_k"), ("maav_c", "L0_attn_maa_v"),
                    ("maaw_c", "L0_attn_maa_w"), ("qb_c", "L0_attn_q_b"),
                    ("kb_c", "L0_attn_k_b"), ("vb_c", "L0_attn_v_b"),
                    ("tdec_c", "L0_attn_time_decay"), ("lng_c", "L0_attn_ln_g"),
                    ("lnb_c", "L0_attn_ln_b"), ("fb_c", "L0_ffn_bias")]:
        g[nm] = _colfm(np.asarray(ins[key], np.float32))
    g["rb1_c"] = np.ascontiguousarray(np.asarray(ins["L0_ffn_rb1"], np.float32)[:, None])
    g["rb2_r"] = np.ascontiguousarray(np.asarray(ins["L0_ffn_rb2"], np.float32)[:16][None])
    g["gat_r"] = np.ascontiguousarray(np.asarray(ins["L0_ffn_gates"], np.float32)[None])
    g["lng_r"] = np.ascontiguousarray(np.asarray(ins["L0_attn_ln_g"], np.float32)[None])
    g["ident"] = np.eye(128, dtype=bfloat16)
    g["trimask"] = (np.arange(64)[None, :] >= (np.arange(128) % 64)[:, None]).astype(bfloat16)
    g["ones_r"] = np.ones((1, 512), dtype=bfloat16)
    g["ones_c"] = np.ones((128, 1), dtype=bfloat16)
    g["repR"] = (np.arange(128)[None, :] % 16 == np.arange(16)[:, None]).astype(bfloat16)
    slabs = []
    for nm in ["Wq", "Wk", "Wv", "Wo", "Wf"]:
        for cc in range(16):
            slabs.append(g[nm][cc])
    slabs.append(g["Wa"][0]); slabs.append(g["Wa"][1])
    for nm in ["Wq", "Wk", "Wv", "Wo", "Wf", "Wa"]:
        del g[nm]
    maps = []
    for c in range(8):
        b, hf = c // 2, c % 2
        t0 = hf * TC
        m = dict(g)
        m["Wsrc"] = np.ascontiguousarray(np.stack(
            [slabs[min(11 * c + i, 81)] for i in range(11)]))
        m["xin"] = np.ascontiguousarray(x[b, t0:t0 + TC])
        m["xprev"] = (np.ascontiguousarray(x[b, t0 - 1:t0]) if t0 > 0
                      else np.zeros((1, C), np.float32))
        m["pmask"] = np.full((128, 1), 1.0 if t0 > 0 else 0.0, np.float32)
        sm = np.zeros((128, 8), np.float32)
        if hf == 1:
            sm[:, c - 1] = 1.0
        m["smask"] = sm
        maps.append(m)
    return maps


def build_program(debug=()):
    nc = bass.Bass("TRN2", target_bir_lowering=False)
    cnt = {}
    sems = {}

    semcms = []

    def mksem(name):
        cm = nc.semaphore(name)
        sems[name] = cm.__enter__()
        semcms.append(cm)
        cnt[name] = 0

    def inc(instr, name, n=1):
        instr.then_inc(sems[name], n)
        cnt[name] += n
        return cnt[name]

    def seminc(eng, name, n=1):
        eng.sem_inc(sems[name], n)
        cnt[name] += n
        return cnt[name]

    dram = {}

    def din(name, shape, dt=F32):
        dram[name] = nc.dram_tensor(name, shape, dt, kind="ExternalInput").ap()

    din("xin", [TC, C]); din("xprev", [1, C]); din("pmask", [128, 1]); din("smask", [128, 8])
    din("Wsrc", [11, 128, 2048])
    din("Wb", [16, 128, 256])
    din("Wmm", [128, 2048]); din("Wm2", [128, 16, 128])
    din("Wt1", [128, 1024]); din("Wt2", [64, 2048])
    din("Wr1", [128, 1024]); din("Wr2", [64, 16])
    VECN = ["maax_c", "maar_c", "maak_c", "maav_c", "maaw_c", "qb_c", "kb_c",
            "vb_c", "tdec_c", "lng_c", "lnb_c", "fb_c"]
    for nm in VECN:
        din(nm, [128, 16])
    din("rb1_c", [64, 1]); din("rb2_r", [1, 16]); din("gat_r", [1, 16]); din("lng_r", [1, 2048])
    din("ident", [128, 128], BF16); din("trimask", [128, 64], BF16)
    din("ones_r", [1, 512], BF16); din("ones_c", [128, 1], BF16); din("repR", [16, 128], BF16)
    out = nc.dram_tensor("out", [TC, C], F32, kind="ExternalOutput").ap()
    dbgt = {}
    for nm, shape, dts in debug:
        dbgt[nm] = nc.dram_tensor(nm, shape, {"f32": F32, "bf16": BF16}[dts],
                                  kind="ExternalOutput").ap()
    sp_y = nc.dram_tensor("sp_y", [128, 16384], BF16, kind="Internal").ap()
    sp_g = nc.dram_tensor("sp_g", [128, 16384], BF16, kind="Internal").ap()
    sp_q = nc.dram_tensor("sp_q", [128, 16384], BF16, kind="Internal").ap()
    conv_in = nc.dram_tensor("conv_in", [11, 128, 2048], BF16, kind="Internal").ap()
    conv_out = nc.dram_tensor("conv_out", [8, 11, 128, 2048], BF16, kind="Internal",
                              addr_space="Shared").ap()
    cc_in = nc.dram_tensor("cc_in", [128, 1024], F32, kind="Internal").ap()
    cc_out = nc.dram_tensor("cc_out", [8, 128, 1024], F32, kind="Internal",
                            addr_space="Shared").ap()
    ctxs = []

    def sbuf(name, shape, dt):
        cm = nc.sbuf_tensor(name, shape, dt)
        h = cm.__enter__()
        ctxs.append(cm)
        return h

    def psum(name, shape, dt):
        cm = nc.psum_tensor(name, shape, dt)
        h = cm.__enter__()
        ctxs.append(cm)
        return h

    SA = sbuf("SA", [128, 16384], BF16)
    SBs = sbuf("SBs", [128, 16384], BF16)
    SC = sbuf("SC", [128, 16384], BF16)
    SD = sbuf("SD", [128, 16384], BF16)
    SE = sbuf("SE", [128, 16384], BF16)
    SF = sbuf("SF", [128, 16384], BF16)
    S_all = sbuf("S_state", [128, 1024], F32)
    Gam = sbuf("Gam", [128, 256], F32)
    goffT = [sbuf("goffA", [128, 256], F32), sbuf("goffB", [128, 256], F32)]
    gcar = sbuf("gcar", [128, 16], F32)
    gb0s = sbuf("gb0s", [128, 32], F32)
    xnb = sbuf("xnb", [128, 16], BF16)
    st1 = sbuf("st1", [128, 16], F32)
    vecs = {nm: sbuf("v_" + nm, [128, 16], F32) for nm in VECN}
    pmsk = sbuf("v_pmask", [128, 1], F32)
    smsk = sbuf("v_smask", [128, 8], F32)
    rb1 = sbuf("rb1s", [64, 1], F32)
    idn = sbuf("idn", [128, 128], BF16)
    tri = sbuf("tris", [128, 64], BF16)
    onr = sbuf("onrs", [1, 512], BF16)
    onc = sbuf("oncs", [128, 1], BF16)
    repR = sbuf("repRs", [16, 128], BF16)
    rb2r = sbuf("rb2rs", [1, 16], BF16)
    gatr = sbuf("gatrs", [1, 16], BF16)
    glbt = sbuf("glbt", [128, 32], BF16)
    epsc = sbuf("epsc", [128, 2], F32)
    wr2s = sbuf("wr2s", [64, 16], BF16)

    PSF = [psum(f"psf{i}", [128, 512], F32) for i in range(6)]
    PSB = [psum(f"psb{i}", [128, 1024], BF16) for i in range(2)]

    for s in ["xw", "xs", "pe", "dv", "ac", "gp", "st", "cc", "sy",
              "wa", "wb", "ia", "ib", "sa", "sb", "va", "vb", "ic", "id"]:
        mksem(s)

    blkcm = nc.Block()
    blk = blkcm.__enter__()

    def _safe(eng, sem):
        class P:
            def __getattr__(self, name):
                f = getattr(eng, name)
                if name in ("wait_ge",) or not callable(f):
                    return f

                def g(*a, **k):
                    eng.wait_ge(sems[sem], cnt[sem])
                    return f(*a, **k)
                return g
        return P()

    def VEC(f):
        @blk.vector
        def _(v):
            f(_safe(v, "dv"))

    def SCA(f):
        @blk.scalar
        def _(sc):
            f(_safe(sc, "ac"))

    def GPS(f):
        @blk.gpsimd
        def _(gp):
            f(_safe(gp, "gp"))

    def v3(slot):
        return slot.ap().rearrange("p (c t) -> p c t", c=16)

    def ch(slot, cc, t0=0, t1=TT):
        return slot[:, cc * 1024 + t0: cc * 1024 + t1]

    def f32v(slot, off, n, p0=0, parts=128):
        return slot.ap().bitcast(F32)[p0:p0 + parts, off: off + n]

    def bfv(slot, off, n, p0=0, parts=128):
        return slot[p0:p0 + parts, off:off + n]

    W0, W1 = 0, 2048
    MMT, TDH, DXC = 4096, 5120, 6144
    XXA, XXB = 7168, 8192
    T2AF, T2BF = 4608, 5632
    WM2 = 13312
    EWF, GCF, CRF, E2F, XPF = 0, 1024, 2048, 3072, 4096
    B_WXA, B_H, B_COF, B_CFF, B_WR2, B_WB, B_CRP = 0, 2048, 3072, 3136, 3648, 3712, 3968
    B_WB2 = 4480
    B_WR1 = 13312
    B_WS0, B_WS1 = 8192, 10240
    LTMF, M1F, M2F, I1F, I2F, LMF, P1F, PDF, P2F, C1F, C2F = \
        6144, 6208, 6212, 6216, 6280, 6344, 6408, 6412, 6416, 6420, 6484
    B_VT, B_KT, B_KF, B_AM = 0, 2048, 4096, 6144
    TS5F = 4096
    B_SIB, B_QG = 0, 1024
    GB0F, GB1F = 1024, 2048
    EXG16 = 5440
    B_LNGR, B_GROW, B_BROW = 0, 2048, 4096
    B_YSQ = 6272
    B_YSQ2 = 15360
    MUF, VAF, RSF, NMF = 3392, 3904, 4416, 4928
    RHB, RLB, NHB, NLB = 10880, 11904, 12928, 13952
    D_XTM = 12288
    D_W0, D_W1 = 0, 2048
    D_IO0F, D_IO1F = 2048, 4096
    S_bf = bfv(SBs, 10240, 1024)
    S_inv = f32v(SBs, 3200, 1024)

    marks = {}

    def barrier():
        @GPS
        def _(gp):
            gp.wait_ge(sems["xw"], cnt["xw"])
            for k in ["va", "vb"]:
                gp.wait_ge(sems[k], cnt[k])
            seminc(gp, "gp")

        @blk.sync
        def _(s):
            s.wait_ge(sems["xs"], cnt["xs"])
            for k in ["ia", "ib", "ic", "id", "sa", "sb", "st", "wa", "wb"]:
                s.wait_ge(sems[k], cnt[k])
            seminc(s, "sy")

        m = {k: cnt[k] for k in ["pe", "dv", "ac", "gp", "sy", "cc"]}

        @blk.tensor
        def _(t):
            for k in ["dv", "ac", "gp", "sy"]:
                t.wait_ge(sems[k], m[k])

        @VEC
        def _(v):
            for k in ["pe", "ac", "gp", "sy"]:
                v.wait_ge(sems[k], m[k])

        @SCA
        def _(sc):
            for k in ["pe", "dv", "gp", "sy"]:
                sc.wait_ge(sems[k], m[k])

        @GPS
        def _(gp):
            for k in ["pe", "dv", "ac", "sy"]:
                gp.wait_ge(sems[k], m[k])

        @blk.sync
        def _(s):
            for k in ["pe", "dv", "ac", "gp"]:
                s.wait_ge(sems[k], m[k])
        return m

    def dump(name, src_ap):
        if name in dbgt:
            barrier()

            @blk.sync
            def _(s):
                inc(s.dma_start(dbgt[name], src_ap), "st", 16)
                s.wait_ge(sems["st"], cnt["st"])
            barrier()

    @blk.sync
    def _(s):
        for nm in VECN:
            inc(s.dma_start(vecs[nm][:], dram[nm]), "xs", 16)
        inc(s.dma_start(pmsk[:], dram["pmask"]), "xs", 16)
        inc(s.dma_start(smsk[:], dram["smask"]), "xs", 16)
        inc(s.dma_start(rb1[:], dram["rb1_c"]), "xs", 16)
        inc(s.dma_start(idn[:], dram["ident"]), "xs", 16)
        inc(s.dma_start(tri[:], dram["trimask"]), "xs", 16)
        inc(s.dma_start(onr[:], dram["ones_r"]), "xs", 16)
        inc(s.dma_start(onc[:], dram["ones_c"]), "xs", 16)
        inc(s.dma_start(repR[:], dram["repR"]), "xs", 16)

    @GPS
    def _(gp):
        inc(gp.dma_start(rb2r[:], dram["rb2_r"]), "xw", 16)
        inc(gp.dma_start(gatr[:], dram["gat_r"]), "xw", 16)
        inc(gp.dma_start(wr2s[:], dram["Wr2"]), "xw", 16)
        inc(gp.dma_start(bfv(SF, W0, 2048), dram["Wmm"]), "xw", 16)
        inc(gp.dma_start(bfv(SF, W1, 1024), dram["Wt1"]), "xw", 16)
        inc(gp.dma_start(bfv(SF, WM2, 2048),
                         dram["Wm2"].rearrange("p c k -> p (c k)")), "xw", 16)
        gp.memset(epsc[:, 0:1], 1e-5)
        gp.memset(epsc[:, 1:2], LN8)
        gp.memset(S_all[:], 0.0)
        inc(gp.memset(gcar[:], 0.0), "gp")
    barrier()

    @blk.sync
    def _(s):
        inc(s.dma_start(f32v(SD, 0, 2048, parts=1), dram["xprev"]), "xs", 16)
        s.wait_ge(sems["xs"], cnt["xs"])
        seminc(s, "sy")
    c1 = cnt["sy"]

    @VEC
    def _(v):
        v.wait_ge(sems["sy"], c1)
        inc(v.reduce_sum(st1[0:1, 0:1], f32v(SD, 0, 2048, parts=1), axis=AX.X), "dv")
        inc(v.tensor_tensor(f32v(SD, 2048, 2048, parts=1), f32v(SD, 0, 2048, parts=1),
                            f32v(SD, 0, 2048, parts=1), OP.mult), "dv")
        inc(v.reduce_sum(st1[0:1, 1:2], f32v(SD, 2048, 2048, parts=1), axis=AX.X), "dv")
        inc(v.tensor_scalar(st1[0:1, 2:3], st1[0:1, 0:1], 1.0 / 2048, None, OP.mult), "dv")
        inc(v.tensor_scalar(st1[0:1, 3:4], st1[0:1, 1:2], 1.0 / 2048, None, OP.mult), "dv")
        inc(v.tensor_tensor(st1[0:1, 4:5], st1[0:1, 2:3], st1[0:1, 2:3], OP.mult), "dv")
        inc(v.tensor_tensor(st1[0:1, 3:4], st1[0:1, 3:4], st1[0:1, 4:5], OP.subtract), "dv")
    c2 = cnt["dv"]

    @SCA
    def _(sc):
        sc.wait_ge(sems["dv"], c2)
        inc(sc.activation(st1[0:1, 5:6], st1[0:1, 3:4], AF.Sqrt,
                          bias=epsc[0:1, 0:1], scale=1.0), "ac")
    c3 = cnt["ac"]

    @VEC
    def _(v):
        v.wait_ge(sems["ac"], c3)
        inc(v.reciprocal(st1[0:1, 6:7], st1[0:1, 5:6]), "dv")
        inc(v.tensor_scalar(bfv(SD, 8192, 2048, parts=1), f32v(SD, 0, 2048, parts=1),
                            st1[0:1, 2:3], st1[0:1, 6:7], OP.subtract, OP.mult), "dv")
    c4 = cnt["dv"]

    @blk.tensor
    def _(t):
        t.wait_ge(sems["dv"], c4)
        for cc2 in range(16):
            i = t.matmul(PSB[0][:, 2 * cc2:2 * cc2 + 1], bfv(SD, 8192 + cc2 * 128, 128, parts=1),
                         idn[0:1, 0:1], start=True, stop=True, is_transpose=True)
        inc(i, "pe")
    c5 = cnt["pe"]

    @VEC
    def _(v):
        v.wait_ge(sems["pe"], c5)
        inc(v.tensor_scalar(xnb[:], PSB[0][:, 0:32:2], pmsk[:, 0:1], None, OP.mult), "dv")
    barrier()

    ev_c_g = {}

    def phase_S1(ti):
        NS = min(NSLAB, S1N)
        ac0 = cnt["ac"]
        sqAc = {k: ac0 + 2 * k + 1 for k in range(NS)}
        evAc = {k: ac0 + 2 * k + 2 for k in range(NS)}
        ld_c, sq_c, red_c, va_c, xtm_c, tp_c, ev_c = {}, {}, {}, {}, {}, {}, {}

        def bufs(sl):
            return (sl % 4, sl % 2, ["ia", "ib", "ic", "id"][sl % 4])

        for sl in range(NS):
            row0 = ti * TT + sl * 128
            buf, sb2, lsem = bufs(sl)
            xld = f32v(SD, buf * 2048, 2048)
            xsq = f32v(SE, sb2 * 2048, 2048)

            @blk.sync
            def _(s):
                s.wait_ge(sems["dv"], xtm_c.get(sl - 4, 0))
                inc(s.dma_start(xld, dram["xin"][row0:row0 + 128, :]), lsem, 16)
            ld_c[sl] = cnt[lsem]

            @GPS
            def _(gp):
                gp.wait_ge(sems[lsem], ld_c[sl])
                gp.wait_ge(sems["dv"], red_c.get(sl - 2, 0))
                inc(gp.tensor_mul(xsq, xld, xld), "gp")
            sq_c[sl] = cnt["gp"]

            so = sb2 * 8
            @VEC
            def _(v):
                v.wait_ge(sems[lsem], ld_c[sl])
                v.wait_ge(sems["ac"], sqAc[sl - 2] if sl >= 2 else 0)
                inc(v.reduce_sum(st1[:, so:so + 1], xld, axis=AX.X), "dv")
                v.wait_ge(sems["gp"], sq_c[sl])
                inc(v.reduce_sum(st1[:, so + 1:so + 2], xsq, axis=AX.X), "dv")
            red_c[sl] = cnt["dv"]

            pv = sl - 1
            if pv >= 0:
                emit_stats(pv, ti, va_c, xtm_c, tp_c, sqAc, evAc, bufs)
        emit_stats(NS - 1, ti, va_c, xtm_c, tp_c, sqAc, evAc, bufs)

        @blk.scalar
        def _(sc):
            for k in range(NS):
                sc.wait_ge(sems["dv"], va_c[k])
                so2 = (k % 2) * 8
                i = sc.activation(st1[:, so2 + 5:so2 + 6], st1[:, so2 + 3:so2 + 4],
                                  AF.Sqrt, bias=epsc[:, 0:1], scale=1.0)
                inc(i, "ac")
                assert cnt["ac"] == sqAc[k]
                sc.wait_ge(sems["pe"], tp_c[k])
                dst = v3(SA)[:, 8:16, k * 128:(k + 1) * 128]
                src2 = PSB[1].ap().rearrange("p (j t) -> p j t", j=8)
                i = sc.activation(dst, src2, AF.Identity, scale=1.0)
                inc(i, "ac")
                assert cnt["ac"] == evAc[k]
        barrier()

    def emit_stats(pv, ti, va_c, xtm_c, tp_c, sqAc, evAc, bufs):
        if pv in va_c:
            return
        buf, sb2, lsem = bufs(pv)
        so = sb2 * 8
        xld = f32v(SD, buf * 2048, 2048)
        xtm = bfv(SE, 8192 + sb2 * 2048, 2048)

        @VEC
        def _(v):
            inc(v.tensor_scalar(st1[:, so + 2:so + 3], st1[:, so:so + 1],
                                1.0 / 2048, None, OP.mult), "dv")
            inc(v.tensor_scalar(st1[:, so + 3:so + 4], st1[:, so + 1:so + 2],
                                1.0 / 2048, None, OP.mult), "dv")
            inc(v.tensor_tensor(st1[:, so + 4:so + 5], st1[:, so + 2:so + 3],
                                st1[:, so + 2:so + 3], OP.mult), "dv")
            inc(v.tensor_tensor(st1[:, so + 3:so + 4], st1[:, so + 3:so + 4],
                                st1[:, so + 4:so + 5], OP.subtract), "dv")
        va_c[pv] = cnt["dv"]

        @VEC
        def _(v):
            v.wait_ge(sems["ac"], sqAc[pv])
            inc(v.reciprocal(st1[:, so + 6:so + 7], st1[:, so + 5:so + 6]), "dv")
            if pv >= 2:
                v.wait_ge(sems["pe"], tp_c[pv - 2])
            inc(v.tensor_scalar(xtm, xld, st1[:, so + 2:so + 3],
                                st1[:, so + 6:so + 7], OP.subtract, OP.mult), "dv")
        xtm_c[pv] = cnt["dv"]

        for half in range(2):
            bb = half
            @blk.tensor
            def _(t):
                t.wait_ge(sems["dv"], xtm_c[pv])
                if pv >= 1:
                    if half == 0:
                        t.wait_ge(sems["dv"], ev_c_g.get(pv - 1, 0))
                    else:
                        t.wait_ge(sems["ac"], evAc[pv - 1])
                for j in range(8):
                    cc = half * 8 + j
                    i = t.matmul(PSB[bb][:, j * 128:(j + 1) * 128],
                                 xtm[:, cc * 128:(cc + 1) * 128], idn[:],
                                 start=True, stop=True, is_transpose=True)
                inc(i, "pe")
            if half == 1:
                tp_c[pv] = cnt["pe"]

            if half == 0:
                @VEC
                def _(v):
                    v.wait_ge(sems["pe"], cnt["pe"])
                    dst = v3(SA)[:, 0:8, pv * 128:(pv + 1) * 128]
                    src2 = PSB[0].ap().rearrange("p (j t) -> p j t", j=8)
                    inc(v.tensor_copy(dst, src2), "dv")
                ev_c_g[pv] = cnt["dv"]

    def emit_dx_full(v, cc):
        xn = v3(SA)
        inc(v.tensor_tensor(ch(SE, cc, 0, 1), xnb[:, cc:cc + 1],
                            xn[:, cc, 0:1], OP.subtract), "dv")
        inc(v.tensor_tensor(ch(SE, cc, 1, 1024), xn[:, cc, 0:1023],
                            xn[:, cc, 1:1024], OP.subtract), "dv")

    def phase_S2(ti):
        xn = v3(SA)

        @GPS
        def _(gp):
            if ti > 0:
                inc(gp.dma_start(bfv(SF, W0, 2048), dram["Wmm"]), "xw", 16)
                inc(gp.dma_start(bfv(SF, W1, 1024), dram["Wt1"]), "xw", 16)
                inc(gp.dma_start(bfv(SF, WM2, 2048),
                                 dram["Wm2"].rearrange("p c k -> p (c k)")), "xw", 16)
                gp.wait_ge(sems["xw"], cnt["xw"])
            seminc(gp, "gp")
        wl_c = cnt["gp"]

        pe_after = {}
        for cc in range(16):
            xb = [XXA, XXB][cc % 2]

            @VEC
            def _(v):
                if cc >= 2:
                    v.wait_ge(sems["pe"], pe_after[cc - 2])
                emit_dx_full(v, cc)
                i = v.scalar_tensor_tensor(bfv(SF, xb, 1024), ch(SE, cc),
                                           vecs["maax_c"][:, cc:cc + 1],
                                           xn[:, cc, :], OP.mult, OP.add)
                inc(i, "dv")
            dv_c = cnt["dv"]

            @blk.tensor
            def _(t):
                t.wait_ge(sems["dv"], dv_c)
                if cc == 0:
                    t.wait_ge(sems["gp"], wl_c)
                for tt in range(2):
                    i = t.matmul(PSF[tt][:, 0:512], bfv(SF, W0 + cc * 128, 128),
                                 bfv(SF, xb + tt * 512, 512),
                                 start=(cc == 0), stop=(cc == 15))
                inc(i, "pe")
            pe_after[cc] = cnt["pe"]
        mm_pe = cnt["pe"]

        @SCA
        def _(sc):
            sc.wait_ge(sems["pe"], mm_pe)
            for tt in range(2):
                inc(sc.activation(bfv(SF, MMT + tt * 512, 512), PSF[tt][:, 0:512],
                                  AF.Tanh), "ac")
        barrier()

        @GPS
        def _(gp):
            inc(gp.dma_start(bfv(SF, W0, 2048, parts=64), dram["Wt2"]), "xw", 16)
            gp.wait_ge(sems["xw"], cnt["xw"])
            seminc(gp, "gp")
        wt2_c = cnt["gp"]

        last = {}
        um_c = {}
        stt_c = {}
        xw_c = {}
        for it in range(18):
            cc = it
            if cc < 16:
                ub = [2, 4][cc % 2]
                t2 = [T2AF, T2BF][cc % 2]

                @blk.tensor
                def _(t):
                    if cc >= 2:
                        t.wait_ge(sems["dv"], stt_c[cc - 2])
                    for tt in range(2):
                        i = t.matmul(PSF[ub + tt][:, 0:512],
                                     bfv(SF, WM2 + cc * 128, 128, p0=96, parts=32),
                                     bfv(SF, MMT + tt * 512, 512, p0=96, parts=32),
                                     start=True, stop=True, tile_position=(96, 0))
                    inc(i, "pe")
                um_c[cc] = cnt["pe"]

                @VEC
                def _(v):
                    v.wait_ge(sems["pe"], um_c[cc])
                    if cc >= 2:
                        v.wait_ge(sems["gp"], xw_c[cc - 2])
                    for tt in range(2):
                        i = v.scalar_tensor_tensor(
                            f32v(SF, t2 + tt * 512, 512),
                            PSF[ub + tt][:, 0:512], vecs["maaw_c"][:, cc:cc + 1],
                            ch(SE, cc, tt * 512, tt * 512 + 512), OP.add, OP.mult)
                        inc(i, "dv")
                stt_c[cc] = cnt["dv"]

            td = it - 2
            if 0 <= td < 16:
                @blk.tensor
                def _(t):
                    t.wait_ge(sems["gp"], xw_c[td])
                    if td == 0:
                        t.wait_ge(sems["gp"], wt2_c)
                    for tt in range(2):
                        i = t.matmul(PSF[tt][0:64, 0:512],
                                     bfv(SF, W1 + td * 64, 64),
                                     bfv(SF, [XXA, XXB][td % 2] + tt * 512, 512),
                                     start=(td == 0), stop=(td == 15))
                    inc(i, "pe")
                last[("td", td)] = cnt["pe"]

            if cc < 16:
                @GPS
                def _(gp):
                    gp.wait_ge(sems["dv"], stt_c[cc])
                    if cc >= 2:
                        gp.wait_ge(sems["pe"], last.get(("td", cc - 2), 0))
                    for tt in range(2):
                        i = gp.tensor_tensor(bfv(SF, [XXA, XXB][cc % 2] + tt * 512, 512),
                                             xn[:, cc, tt * 512:tt * 512 + 512],
                                             f32v(SF, t2 + tt * 512, 512), OP.add)
                        inc(i, "gp")
                xw_c[cc] = cnt["gp"]
        td_pe = cnt["pe"]

        @SCA
        def _(sc):
            sc.wait_ge(sems["pe"], td_pe)
            for tt in range(2):
                inc(sc.activation(bfv(SF, TDH + tt * 512, 512, parts=64),
                                  PSF[tt][0:64, 0:512], AF.Tanh), "ac")
        barrier()
        wmc, vbc, crc, f1c = {}, {}, {}, {}
        ac0 = cnt["ac"]
        ewc, bigc = {}, {}
        acc = ac0
        for it in range(19):
            c2p = it - 2
            if 0 <= c2p < 16:
                acc += 4
                bigc[c2p] = acc
            if it < 16:
                acc += 2
                ewc[it] = acc
        for it in range(19):
            cc = it
            if cc < 16:
                wb = [2, 4][cc % 2]

                @blk.tensor
                def _(t):
                    if cc >= 2:
                        t.wait_ge(sems["ac"], ewc[cc - 2])
                    for tt in range(2):
                        i = t.matmul(PSF[wb + tt][:, 0:512],
                                     bfv(SF, W0 + cc * 128, 128, parts=64),
                                     bfv(SF, TDH + tt * 512, 512, parts=64),
                                     start=True, stop=True)
                    inc(i, "pe")
                wmc[cc] = cnt["pe"]

            c2 = it - 2
            if 0 <= c2 < 16:
                base = 4096 * (c2 % 2)
                E2 = base + 3072

                @VEC
                def _(v):
                    v.wait_ge(sems["ac"], bigc[c2])
                    i = v.scalar_tensor_tensor(ch(SD, c2), f32v(SBs, E2, 1024), 1.0,
                                               f32v(SBs, base + 1024, 1024),
                                               OP.subtract, OP.mult)
                    inc(i, "dv")
                f1c[c2] = cnt["dv"]

            c1 = it - 1
            if 0 <= c1 < 16:
                base = 4096 * (c1 % 2)
                EW, GC, CR = base, base + 1024, base + 2048
                gsl = (c1 % 2) * 16

                @VEC
                def _(v):
                    v.wait_ge(sems["ac"], ewc[c1])
                    if c1 >= 2:
                        v.wait_ge(sems["gp"], crc[c1 - 2])
                    inc(v.tensor_scalar(f32v(SBs, EW, 1024), f32v(SBs, EW, 1024),
                                        5.0, -1.0, OP.min, OP.mult), "dv")
                    inc(v.tensor_tensor_scan(f32v(SBs, GC, 1024), f32v(SBs, EW, 1024),
                                             f32v(SBs, EW, 1024), 0.0, OP.add, OP.bypass), "dv")
                    gcv = f32v(SBs, GC, 1024).rearrange("p (n l) -> p n l", n=16)
                    inc(v.tensor_copy(gb0s[:, gsl:gsl + 16].unsqueeze(2),
                                      gcv[:, :, 63:64]), "dv")
                    inc(v.tensor_copy(f32v(SBs, CR, 64), f32v(SBs, GC, 64)), "dv")
                    inc(v.tensor_copy(goffT[ti][:, c1 * 16:c1 * 16 + 1],
                                      gcar[:, c1:c1 + 1]), "dv")
                    inc(v.tensor_scalar(goffT[ti][:, c1 * 16 + 1:c1 * 16 + 16],
                                        gb0s[:, gsl:gsl + 15], gcar[:, c1:c1 + 1],
                                        None, OP.add), "dv")
                    inc(v.tensor_tensor(gcar[:, c1:c1 + 1], gcar[:, c1:c1 + 1],
                                        gb0s[:, gsl + 15:gsl + 16], OP.add), "dv")
                vbc[c1] = cnt["dv"]

                @GPS
                def _(gp):
                    gp.wait_ge(sems["dv"], vbc[c1])
                    if c1 >= 2:
                        gp.wait_ge(sems["ac"], bigc[c1 - 2])
                    gcv = f32v(SBs, GC, 1024).rearrange("p (n l) -> p n l", n=16)
                    crv = f32v(SBs, CR, 1024).rearrange("p (n l) -> p n l", n=16)
                    i = gp.tensor_tensor(crv[:, 1:16, :], gcv[:, 1:16, :],
                                         gb0s[:, gsl:gsl + 15].unsqueeze(2)
                                         .broadcast_to([128, 15, 64]), OP.subtract)
                    inc(i, "gp")
                crc[c1] = cnt["gp"]

        @blk.scalar
        def _(sc):
            for it in range(19):
                c2 = it - 2
                if 0 <= c2 < 16:
                    base = 4096 * (c2 % 2)
                    EW, GC, CR, E2 = base, base + 1024, base + 2048, base + 3072
                    sc.wait_ge(sems["gp"], crc[c2])
                    if c2 >= 2:
                        sc.wait_ge(sems["dv"], f1c[c2 - 2])
                    sc.wait_ge(sems["ac"], cnt["ac"])
                    crv = f32v(SBs, CR, 1024).rearrange("p (n l) -> p n l", n=16)
                    inc(sc.activation(Gam[:, c2 * 16:(c2 + 1) * 16].unsqueeze(2),
                                      crv[:, :, 63:64], AF.Exp, scale=1.0), "ac")
                    inc(sc.activation(ch(SC, c2), f32v(SBs, CR, 1024), AF.Exp,
                                      bias=epsc[:, 1:2], scale=1.0), "ac")
                    inc(sc.activation(f32v(SBs, E2, 1024), f32v(SBs, EW, 1024),
                                      AF.Exp, scale=1.0), "ac")
                    inc(sc.activation(f32v(SBs, GC, 1024), f32v(SBs, CR, 1024),
                                      AF.Exp, scale=-1.0), "ac")
                    assert cnt["ac"] == bigc[c2], (cnt["ac"], bigc[c2])
                cc = it
                if cc < 16:
                    wb = [2, 4][cc % 2]
                    sc.wait_ge(sems["pe"], wmc[cc])
                    if cc >= 2:
                        sc.wait_ge(sems["dv"], vbc[cc - 2])
                    sc.wait_ge(sems["ac"], cnt["ac"])
                    base = 4096 * (cc % 2)
                    for tt in range(2):
                        i = sc.activation(f32v(SBs, base + tt * 512, 512),
                                          PSF[wb + tt][:, 0:512], AF.Exp,
                                          bias=vecs["tdec_c"][:, cc:cc + 1], scale=1.0)
                        inc(i, "ac")
                    assert cnt["ac"] == ewc[cc], (cnt["ac"], ewc[cc])
        barrier()

    def phase_S3(ti, kind):
        xn = v3(SA)
        r, maa, bc, Wn = {"q": (0, "maar_c", "qb_c", "Wq"),
                          "k": (1, "maak_c", "kb_c", "Wk"),
                          "v": (2, "maav_c", "vb_c", "Wv")}[kind]
        dst = {"q": SC, "k": SD, "v": SE}[kind]
        last = {}
        for cc in range(16):
            ub = [2, 4][cc % 2]
            t2 = [T2AF, T2BF][cc % 2]

            @blk.tensor
            def _(t):
                if cc >= 2:
                    t.wait_ge(sems["dv"], last.get(("stt", cc - 2), 0))
                for tt in range(2):
                    i = t.matmul(PSF[ub + tt][:, 0:512],
                                 bfv(SF, WM2 + cc * 128, 128, p0=32 * r, parts=32),
                                 bfv(SF, MMT + tt * 512, 512, p0=32 * r, parts=32),
                                 start=True, stop=True,
                                 tile_position=(32 * r, 0))
                inc(i, "pe")
            um_c = cnt["pe"]

            @VEC
            def _(v):
                v.wait_ge(sems["pe"], um_c)
                if cc >= 2:
                    v.wait_ge(sems["gp"], last.get(("xp", cc - 2), 0))
                for tt in range(2):
                    i = v.scalar_tensor_tensor(
                        f32v(SF, t2 + tt * 512, 512),
                        PSF[ub + tt][:, 0:512], vecs[maa][:, cc:cc + 1],
                        ch(SE, cc, tt * 512, tt * 512 + 512), OP.add, OP.mult)
                    inc(i, "dv")
                i = v.tensor_tensor(bfv(SBs, cc * 1024 + 512, 512),
                                    xn[:, cc, 512:1024],
                                    f32v(SF, t2 + 512, 512), OP.add)
                inc(i, "dv")
            last[("stt", cc)] = cnt["dv"]
            stt_c = cnt["dv"]

            @GPS
            def _(gp):
                gp.wait_ge(sems["dv"], stt_c)
                i = gp.tensor_tensor(bfv(SBs, cc * 1024, 512),
                                     xn[:, cc, 0:512],
                                     f32v(SF, t2, 512), OP.add)
                inc(i, "gp")
            last[("xp", cc)] = cnt["gp"]
        xp_gp = cnt["gp"]

        vAc = {}
        if kind == "v":
            vac0 = cnt["ac"]
            for cc in range(16):
                for tt in range(2):
                    vAc[(cc, tt)] = vac0 + 2 * cc + tt + 1
        vmm = {}
        for cc in range(16):
            wb = [W0, W1][cc % 2]

            @blk.sync
            def _(s):
                s.wait_ge(sems["cc"], conv_cc[0])
                if cc >= 2:
                    s.wait_ge(sems["pe"], last[("mm", cc - 2)])
                inc(s.dma_start(bfv(SF, wb, 2048), conv_slab(SLAB[(Wn, cc)])),
                    ["wa", "wb"][cc % 2], 16)
            wl_c = cnt[["wa", "wb"][cc % 2]]

            for tt in range(2):
                pb = (cc % 2) * 2 + tt
                @blk.tensor
                def _(t):
                    t.wait_ge(sems[["wa", "wb"][cc % 2]], wl_c)
                    t.wait_ge(sems["gp"], xp_gp)
                    t.wait_ge(sems["dv"], last.get(("ev", tt, cc - 2), 0))
                    if kind == "v" and cc >= 2:
                        t.wait_ge(sems["ac"], vAc[(cc - 2, tt)])
                    for kc in range(16):
                        i = t.matmul(PSF[pb][:, 0:512],
                                     bfv(SF, wb + kc * 128, 128),
                                     bfv(SBs, kc * 1024 + tt * 512, 512),
                                     start=(kc == 0), stop=(kc == 15))
                    inc(i, "pe")
                mm_c = cnt["pe"]
                vmm[(cc, tt)] = mm_c
                if tt == 1:
                    last[("mm", cc)] = mm_c

                if kind == "v":
                    pass
                else:
                    @VEC
                    def _(v):
                        v.wait_ge(sems["pe"], mm_c)
                        d = ch(dst, cc, tt * 512, tt * 512 + 512)
                        fac = ch(dst, cc, tt * 512, tt * 512 + 512)
                        i = v.scalar_tensor_tensor(d, PSF[pb][:, 0:512],
                                                   vecs[bc][:, cc:cc + 1], fac,
                                                   OP.add, OP.mult)
                        inc(i, "dv")
                    last[("ev", tt, cc)] = cnt["dv"]
        if kind == "v":
            @blk.scalar
            def _(sc):
                for cc in range(16):
                    for tt in range(2):
                        pb = (cc % 2) * 2 + tt
                        sc.wait_ge(sems["pe"], vmm[(cc, tt)])
                        d = ch(dst, cc, tt * 512, tt * 512 + 512)
                        i = sc.activation(d, PSF[pb][:, 0:512], AF.Identity,
                                          bias=vecs[bc][:, cc:cc + 1], scale=1.0)
                        inc(i, "ac")
                        assert cnt["ac"] == vAc[(cc, tt)]
        barrier()

    def phase_S4(ti):
        xn = v3(SA)
        last = {}
        ac0 = cnt["ac"]
        hAc = {0: ac0 + 1, 1: ac0 + 3}
        sgAc = {0: ac0 + 2, 1: ac0 + 4}
        gAc = {}
        for cc in range(16):
            for tt in range(2):
                gAc[(cc, tt)] = ac0 + 4 + 2 * cc + tt + 1
        pe_pend = {}
        dv_pend = {}

        @GPS
        def _(gp):
            inc(gp.dma_start(bfv(SBs, B_WR1, 1024), dram["Wr1"]), "xw", 16)
            gp.wait_ge(sems["xw"], cnt["xw"])
            seminc(gp, "gp")
        wr1_c = cnt["gp"]

        for tt in range(2):
            @blk.tensor
            def _(t):
                t.wait_ge(sems["gp"], wr1_c)
                t.wait_ge(sems["ac"], hAc[tt - 1] if tt >= 1 else 0)
                for cc in range(16):
                    i = t.matmul(PSF[2][0:64, 0:512],
                                 bfv(SBs, B_WR1 + cc * 64, 64),
                                 xn[:, cc, tt * 512:tt * 512 + 512],
                                 start=(cc == 0), stop=(cc == 15))
                inc(i, "pe")
            pe_pend[("h", tt)] = cnt["pe"]

            @blk.tensor
            def _(t):
                t.wait_ge(sems["ac"], hAc[tt])
                t.wait_ge(sems["dv"], last.get(("ltm", tt - 1), 0))
                for sl in range(4):
                    i = t.matmul(PSF[3][:, sl * 16:sl * 16 + 16],
                                 bfv(SBs, B_H + tt * 512 + sl * 128, 128, parts=64),
                                 wr2s[:], start=True, stop=False)
                    i = t.matmul(PSF[3][:, sl * 16:sl * 16 + 16], onr[:, 0:128],
                                 rb2r[:], start=False, stop=False)
                    i = t.matmul(PSF[3][:, sl * 16:sl * 16 + 16], onr[:, 0:128],
                                 gatr[:], start=False, stop=True)
                inc(i, "pe")
            lg_pe = cnt["pe"]

            @VEC
            def _(v):
                v.wait_ge(sems["pe"], lg_pe)
                ltm = f32v(SBs, LTMF, 64)
                l3 = ltm.rearrange("p (a b) -> p a b", a=4)
                inc(v.tensor_copy(ltm, PSF[3][:, 0:64]), "dv")
                m1 = f32v(SBs, M1F, 4)
                inc(v.tensor_reduce(m1.unsqueeze(2), l3, axis=AX.X, op=OP.max), "dv")
                i1 = f32v(SBs, I1F, 64).rearrange("p (a b) -> p a b", a=4)
                inc(v.tensor_tensor(i1, l3, m1.unsqueeze(2).broadcast_to([128, 4, 16]),
                                    OP.is_ge), "dv")
                lm = f32v(SBs, LMF, 64).rearrange("p (a b) -> p a b", a=4)
                inc(v.scalar_tensor_tensor(lm, f32v(SBs, I1F, 64).rearrange(
                    "p (a b) -> p a b", a=4), -1e30, l3, OP.mult, OP.add), "dv")
                m2 = f32v(SBs, M2F, 4)
                inc(v.tensor_reduce(m2.unsqueeze(2), lm, axis=AX.X, op=OP.max), "dv")
                i2 = f32v(SBs, I2F, 64).rearrange("p (a b) -> p a b", a=4)
                inc(v.tensor_tensor(i2, l3, m2.unsqueeze(2).broadcast_to([128, 4, 16]),
                                    OP.is_ge), "dv")
                inc(v.tensor_tensor(f32v(SBs, P1F, 4), m1, m2, OP.subtract), "dv")
            dv_pend[("cf", tt)] = cnt["dv"]

            @VEC
            def _(v):
                v.wait_ge(sems["ac"], sgAc[tt])
                p1 = f32v(SBs, P1F, 4)
                inc(v.tensor_scalar(f32v(SBs, PDF, 4), p1, 2.0, -1.0, OP.mult, OP.add), "dv")
                inc(v.tensor_scalar(f32v(SBs, P2F, 4), p1, -1.0, 1.0, OP.mult, OP.add), "dv")
                i2 = f32v(SBs, I2F, 64).rearrange("p (a b) -> p a b", a=4)
                i1 = f32v(SBs, I1F, 64).rearrange("p (a b) -> p a b", a=4)
                c1 = f32v(SBs, C1F, 64).rearrange("p (a b) -> p a b", a=4)
                c2 = f32v(SBs, C2F, 64).rearrange("p (a b) -> p a b", a=4)
                inc(v.tensor_tensor(c1, i2, f32v(SBs, P2F, 4).unsqueeze(2)
                                    .broadcast_to([128, 4, 16]), OP.mult), "dv")
                inc(v.tensor_tensor(c2, i1, f32v(SBs, PDF, 4).unsqueeze(2)
                                    .broadcast_to([128, 4, 16]), OP.mult), "dv")
                inc(v.tensor_tensor(bfv(SBs, B_COF, 64), f32v(SBs, C1F, 64),
                                    f32v(SBs, C2F, 64), OP.add), "dv")
            co_c = cnt["dv"]

            @blk.tensor
            def _(t):
                t.wait_ge(sems["dv"], co_c)
                for sl in range(4):
                    i = t.matmul(PSB[0][0:16, sl * 128:sl * 128 + 128],
                                 bfv(SBs, B_COF + sl * 16, 16), idn[:],
                                 start=True, stop=True, is_transpose=True)
                inc(i, "pe")
            ct_pe = cnt["pe"]

            @VEC
            def _(v):
                v.wait_ge(sems["pe"], ct_pe)
                inc(v.tensor_copy(bfv(SBs, B_CFF, 512, parts=16), PSB[0][0:16, 0:512]), "dv")
            cff_c = cnt["dv"]

            if tt == 0:
                @blk.sync
                def _(s):
                    s.wait_ge(sems["cc"], conv_cc[0])
                    inc(s.dma_start(bfv(SBs, B_WS0, 2048), conv_slab(80)), "wa", 16)
                    inc(s.dma_start(bfv(SBs, B_WS1, 2048), conv_slab(81)), "wb", 16)
            wa_c = cnt["wa"]
            wb2_c = cnt["wb"]

            @blk.tensor
            def _(t):
                t.wait_ge(sems["dv"], cff_c)
                t.wait_ge(sems["wa"], wa_c)
                t.wait_ge(sems["wb"], wb2_c)
                t.wait_ge(sems["dv"], last.get(("wxa", tt - 1), 0))
                i = t.matmul(PSF[4][:, 0:512], repR[:], bfv(SBs, B_CFF, 512, parts=16),
                             start=True, stop=True)
                inc(i, "pe")
            crep_pe = cnt["pe"]

            @VEC
            def _(v):
                v.wait_ge(sems["pe"], crep_pe)
                inc(v.tensor_copy(bfv(SBs, B_CRP, 512), PSF[4][:, 0:512]), "dv")
            crep_dv = cnt["dv"]

            @blk.tensor
            def _(t):
                for kc in range(2):
                    ws = [B_WS0, B_WS1][kc]
                    for cin in range(16):
                        i = t.matmul(PSF[kc][:, 0:512], bfv(SBs, ws + cin * 128, 128),
                                     xn[:, cin, tt * 512:tt * 512 + 512],
                                     start=(cin == 0), stop=(cin == 15))
                    inc(i, "pe")
            xa_pe = cnt["pe"]

            @VEC
            def _(v):
                v.wait_ge(sems["pe"], xa_pe)
                v.wait_ge(sems["dv"], crep_dv)
                for kc in range(2):
                    i = v.tensor_tensor(bfv(SBs, B_WXA + kc * 1024 + tt * 512, 512),
                                        PSF[kc][:, 0:512], bfv(SBs, B_CRP, 512), OP.mult)
                    inc(i, "dv")
            last[("wxa", tt)] = cnt["dv"]
            last[("ltm", tt)] = cnt["dv"]
        wxa_dv = cnt["dv"]

        g_pend = {}
        for cc in range(16):
            wb = [B_WS0, B_WS1][cc % 2]

            @blk.sync
            def _(s):
                s.wait_ge(sems["dv"], wxa_dv)
                if cc >= 2:
                    s.wait_ge(sems["ac"], gAc[(cc - 2, 1)])
                inc(s.dma_start(bfv(SBs, wb, 2048), conv_slab(SLAB[("Wf", cc)])),
                    ["wa", "wb"][cc % 2], 16)
            wf_c = cnt[["wa", "wb"][cc % 2]]

            @GPS
            def _(gp):
                gp.wait_ge(sems["dv"], wxa_dv)
                if cc >= 2:
                    gp.wait_ge(sems["ac"], gAc[(cc - 2, 1)])
                inc(gp.dma_start(bfv(SBs, [B_WB, B_WB2][cc % 2], 256), dram["Wb"][cc]),
                    ["va", "vb"][cc % 2], 16)
            vf_c = cnt[["va", "vb"][cc % 2]]

            for tt in range(2):
                pb = (cc % 2) * 2 + tt
                @blk.tensor
                def _(t):
                    t.wait_ge(sems[["wa", "wb"][cc % 2]], wf_c)
                    t.wait_ge(sems[["va", "vb"][cc % 2]], vf_c)
                    t.wait_ge(sems["ac"], gAc[(cc - 2, tt)] if cc >= 2 else 0)
                    if cc < 2:
                        t.wait_ge(sems["dv"], wxa_dv)
                    for kc in range(16):
                        t.matmul(PSF[pb][:, 0:512], bfv(SBs, wb + kc * 128, 128),
                                 xn[:, kc, tt * 512:tt * 512 + 512],
                                 start=(kc == 0), stop=False)
                    for kc in range(2):
                        i = t.matmul(PSF[pb][:, 0:512],
                                     bfv(SBs, [B_WB, B_WB2][cc % 2] + kc * 128, 128),
                                     bfv(SBs, B_WXA + kc * 1024 + tt * 512, 512),
                                     start=False, stop=(kc == 1))
                    inc(i, "pe")
                g_pend[(cc, tt)] = cnt["pe"]

        @blk.scalar
        def _(sc):
            for tt in range(2):
                sc.wait_ge(sems["pe"], pe_pend[("h", tt)])
                inc(sc.activation(bfv(SBs, B_H + tt * 512, 512, parts=64),
                                  PSF[2][0:64, 0:512], AF.Silu,
                                  bias=rb1[:, 0:1], scale=1.0), "ac")
                assert cnt["ac"] == hAc[tt]
                sc.wait_ge(sems["dv"], dv_pend[("cf", tt)])
                inc(sc.activation(f32v(SBs, P1F, 4), f32v(SBs, P1F, 4),
                                  AF.Sigmoid, scale=1.0), "ac")
                assert cnt["ac"] == sgAc[tt]
            for cc in range(16):
                for tt in range(2):
                    pb = (cc % 2) * 2 + tt
                    sc.wait_ge(sems["pe"], g_pend[(cc, tt)])
                    i = sc.activation(ch(SF, cc, tt * 512, tt * 512 + 512),
                                      PSF[pb][:, 0:512], AF.Silu,
                                      bias=vecs["fb_c"][:, cc:cc + 1], scale=1.0)
                    inc(i, "ac")
                    assert cnt["ac"] == gAc[(cc, tt)]
        barrier()

    def c3(slot, plo, pn, cc, t0, t1):
        return v3(slot)[plo:plo + pn, cc:cc + 1, t0:t1].squeeze(1)

    def phase_S5(ti):
        qt, kt, vt = SC, SD, SE
        last = {}
        ac_base = cnt["ac"]
        evac_ods = {}
        acev = {}

        @VEC
        def _(v):
            inc(v.tensor_copy(S_bf, S_all[:]), "dv")
        last[("sbf", -1)] = cnt["dv"]
        for m in range(8):
            p0 = m * 128

            @GPS
            def _(gp):
                if m >= 1:
                    gp.wait_ge(sems["pe"], last[("ktp", m - 1)])
                kv = v3(kt)[:, :, p0:p0 + 128].rearrange("p c (n l) -> p c n l", n=2)
                gv = Gam.ap().rearrange("p (c n) -> p c n", c=16)[:, :, 2 * m:2 * m + 2]
                i = gp.tensor_tensor(bfv(SBs, B_KF, 2048).rearrange(
                    "p (c n l) -> p c n l", c=16, n=2), kv,
                    gv.unsqueeze(3).broadcast_to([128, 16, 2, 64]), OP.mult)
                inc(i, "gp")
            kf_c = cnt["gp"]

            if S5CUT < 2:
                continue
            for (src_is_v, rnd) in [(True, 0), (True, 1), (False, 0), (False, 1)]:
                bb = rnd
                @blk.tensor
                def _(t):
                    if not src_is_v:
                        t.wait_ge(sems["gp"], kf_c)
                    t.wait_ge(sems["dv"], last.get(("tev", bb), 0))
                    for j in range(8):
                        hp = rnd * 8 + j
                        if src_is_v:
                            srcv = c3(vt, 0, 128, hp, p0, p0 + 128)
                        else:
                            srcv = bfv(SBs, B_KF + hp * 128, 128)
                        i = t.matmul(PSB[bb][:, j * 128:(j + 1) * 128], srcv, idn[:],
                                     start=True, stop=True, is_transpose=True)
                    inc(i, "pe")
                tp_c = cnt["pe"]
                if (not src_is_v) and rnd == 1:
                    last[("ktp", m)] = tp_c

                @VEC
                def _(v):
                    v.wait_ge(sems["pe"], tp_c)
                    dstoff = (B_VT if src_is_v else B_KT) + rnd * 1024
                    inc(v.tensor_copy(bfv(SBs, dstoff, 1024), PSB[bb][:, 0:1024]), "dv")
                last[("tev", bb)] = cnt["dv"]
            tm_dv = cnt["dv"]

            def tmoff(base, hp_, pr):
                return base + (hp_ // 8) * 1024 + (hp_ % 8) * 128 + pr * 64

            if S5CUT < 3:
                continue
            @blk.tensor
            def _(t):
                t.wait_ge(sems["ac"], acev.get(2 * m - 1, ac_base))
                t.wait_ge(sems["dv"], last.get(("sub", 2 * m - 1), 0))
                for n in (0, 1):
                    for h in range(32):
                        hp_, pr = h // 2, h % 2
                        bank = 2 * (h % 2) + h // 16
                        slot = (h % 16) // 2
                        i = t.matmul(
                            PSF[bank][n * 64:n * 64 + 64, slot * 64:slot * 64 + 64],
                            c3(kt, pr * 64, 64, hp_, p0 + n * 64, p0 + n * 64 + 64),
                            c3(qt, pr * 64, 64, hp_, p0 + n * 64, p0 + n * 64 + 64),
                            start=True, stop=True, tile_position=(pr * 64, n * 64))
                inc(i, "pe")
            b_pe = cnt["pe"]

            if S5CUT < 4:
                continue
            @VEC
            def _(v):
                v.wait_ge(sems["pe"], b_pe)
                for bank in range(4):
                    i = v.tensor_tensor(
                        bfv(SBs, B_AM + bank * 512, 512).rearrange("p (s l) -> p s l", s=8),
                        PSF[bank].ap().rearrange("p (s l) -> p s l", s=8),
                        tri.ap().unsqueeze(1).broadcast_to([128, 8, 64]), OP.mult)
                    inc(i, "dv")
            am_dv = cnt["dv"]

            if S5CUT < 5:
                continue
            for n2 in (0, 1):
                n = 2 * m + n2
                @blk.tensor
                def _(t):
                    t.wait_ge(sems["dv"], am_dv)
                    t.wait_ge(sems["dv"], tm_dv)
                    t.wait_ge(sems["ac"], acev.get(n - 1, ac_base))
                    t.wait_ge(sems["dv"], last.get(("sub", n - 1), 0))
                    t.wait_ge(sems["dv"], last.get(("sbf", n - 1), 0))
                    for h in range(32):
                        hp_, pr = h // 2, h % 2
                        bank, slot = h // 16, (h % 16) // 2
                        amv = bfv(SBs, B_AM + (2 * (h % 2) + h // 16) * 512
                                  + ((h % 16) // 2) * 64, 64, p0=n2 * 64, parts=64)
                        vtv = bfv(SBs, tmoff(B_VT, hp_, pr), 64, p0=n2 * 64, parts=64)
                        t.matmul(PSF[bank][pr * 64:pr * 64 + 64, slot * 64:slot * 64 + 64],
                                 vtv, amv, start=True, stop=True,
                                 tile_position=(n2 * 64, pr * 64))
                    for h in range(32):
                        hp_, pr = h // 2, h % 2
                        bank, slot = 2 + h // 16, (h % 16) // 2
                        qv = c3(qt, pr * 64, 64, hp_, p0 + n2 * 64, p0 + n2 * 64 + 64)
                        sv = S_bf.rearrange("p (s l) -> p s l", s=16)[
                            pr * 64:pr * 64 + 64, hp_:hp_ + 1, :].squeeze(1)
                        i = t.matmul(PSF[bank][pr * 64:pr * 64 + 64, slot * 64:slot * 64 + 64],
                                     sv, qv, start=True, stop=True,
                                     tile_position=(pr * 64, pr * 64))
                    inc(i, "pe")
                    for h in range(32):
                        hp_, pr = h // 2, h % 2
                        bank, slot = 4 + h // 16, (h % 16) // 2
                        ktv = bfv(SBs, tmoff(B_KT, hp_, pr), 64, p0=n2 * 64, parts=64)
                        i = t.matmul(PSF[bank][pr * 64:pr * 64 + 64, slot * 64:slot * 64 + 64],
                                     ktv, bfv(SBs, tmoff(B_VT, hp_, pr), 64,
                                              p0=n2 * 64, parts=64),
                                     start=True, stop=True,
                                     tile_position=(n2 * 64, pr * 64))
                    inc(i, "pe")
                ods_pe = cnt["pe"]

                @GPS
                def _(gp):
                    gp.wait_ge(sems["dv"], last.get(("supd", n - 1), 0))
                    gv = Gam.ap().rearrange("p (c n) -> p c n", c=16)[:, :, n:n + 1]
                    i = gp.tensor_tensor(
                        f32v(SBs, TS5F, 1024).rearrange("p (s l) -> p s l", s=16),
                        S_all.ap().rearrange("p (s l) -> p s l", s=16),
                        gv.broadcast_to([128, 16, 64]), OP.mult)
                    inc(i, "gp")
                ts_c = cnt["gp"]

                evac_ods[n] = ods_pe
                acev[n] = ac_base + 2 * n + 2

                @VEC
                def _(v):
                    v.wait_ge(sems["pe"], ods_pe)
                    v.wait_ge(sems["gp"], ts_c)
                    for bank in range(2):
                        i = v.tensor_tensor(
                            S_all[:, bank * 512:bank * 512 + 512],
                            f32v(SBs, TS5F + bank * 512, 512),
                            PSF[4 + bank][:, 0:512], OP.add)
                        inc(i, "dv")
                    supd_c = cnt["dv"]
                    last[("supd", n)] = supd_c
                    inc(v.tensor_copy(S_bf, S_all[:]), "dv")
                    last[("sbf", n)] = cnt["dv"]
                    v.wait_ge(sems["ac"], acev[n])
                    yv = v3(SA)
                    for bank in range(2):
                        i = v.tensor_tensor(
                            yv[:, bank * 8:bank * 8 + 8, n * 64:n * 64 + 64],
                            yv[:, bank * 8:bank * 8 + 8, n * 64:n * 64 + 64],
                            PSF[2 + bank].ap().rearrange("p (s l) -> p s l", s=8),
                            OP.subtract)
                        inc(i, "dv")
                last[("sub", n)] = cnt["dv"]
        @blk.scalar
        def _(sc):
            for n in sorted(evac_ods):
                sc.wait_ge(sems["pe"], evac_ods[n])
                yv = v3(SA)
                for bank in range(2):
                    i = sc.activation(
                        yv[:, bank * 8:bank * 8 + 8, n * 64:n * 64 + 64],
                        PSF[bank].ap().rearrange("p (s l) -> p s l", s=8),
                        AF.Identity, scale=-1.0)
                    inc(i, "ac")
                assert cnt["ac"] == acev[n]
        barrier()

    expg_ac = {}

    def phase_expg():
        ac0 = cnt["ac"]
        for j, ti in enumerate((1, 0)):
            for cc in range(16):
                expg_ac[(ti, cc)] = ac0 + j * 16 + cc + 1

        @blk.scalar
        def _(sc):
            for ti in (1, 0):
                for cc in range(16):
                    i = sc.activation(f32v(SBs, EXG16 + ti * 256 + cc * 16, 16),
                                      goffT[ti][:, cc * 16:(cc + 1) * 16], AF.Exp,
                                      scale=1.0)
                    inc(i, "ac")
                    assert cnt["ac"] == expg_ac[(ti, cc)]

    def phase_exchange():
        @GPS
        def _(gp):
            inc(gp.memset(S_inv, 0.0), "gp")
            inc(gp.dma_start(SE[:, :], sp_y), "va", 16)
        z_c = cnt["gp"]
        spy_rl = cnt["va"]

        @blk.sync
        def _(s):
            inc(s.dma_start(cc_in, S_all[:]), "xs", 16)
            s.wait_ge(sems["xs"], cnt["xs"])
            seminc(s, "sy")
        ci_c = cnt["sy"]

        @GPS
        def _(gp):
            gp.wait_ge(sems["sy"], ci_c)
            i = gp.collective_compute("AllGather", OP.bypass,
                                      replica_groups=[list(range(8))],
                                      ins=[cc_in], outs=[cc_out])
            inc(i, "cc")
        cc_c = cnt["cc"]

        last = 0
        for j, r in enumerate((0, 2, 4, 6)):
            gb = [GB0F, GB1F][j % 2]

            @blk.sync
            def _(s):
                s.wait_ge(sems["cc"], cc_c)
                s.wait_ge(sems["dv"], last)
                inc(s.dma_start(f32v(SBs, gb, 1024), cc_out[r]), ["ia", "ib"][j % 2], 16)
            g_c = cnt[["ia", "ib"][j % 2]]

            @VEC
            def _(v):
                v.wait_ge(sems[["ia", "ib"][j % 2]], g_c)
                v.wait_ge(sems["gp"], z_c)
                i = v.scalar_tensor_tensor(S_inv, f32v(SBs, gb, 1024),
                                           smsk[:, r:r + 1], S_inv, OP.mult, OP.add)
                inc(i, "dv")
            last = cnt["dv"]

        @VEC
        def _(v):
            inc(v.tensor_copy(bfv(SBs, B_SIB, 1024), S_inv), "dv")
        return spy_rl

    def phase_corr(ti, yslot, qslot, rl_in=None, rl_out=None, ysub=None):
        qg_c = {}
        rl_pend = {}
        for cc in range(16):
            @GPS
            def _(gp):
                gp.wait_ge(sems["ac"], expg_ac[(ti, cc)])
                gp.wait_ge(sems["pe"], cnt["pe"])
                if rl_in is not None:
                    sm, val = rl_in[cc]
                    gp.wait_ge(sems[sm], val)
                qv = v3(qslot)[:, cc:cc + 1, :].squeeze(1).rearrange("p (n l) -> p n l", n=16)
                i = gp.tensor_tensor(bfv(SBs, B_QG, 1024).rearrange("p (n l) -> p n l", n=16),
                                     qv, f32v(SBs, EXG16 + ti * 256 + cc * 16, 16)
                                     .unsqueeze(2).broadcast_to([128, 16, 64]), OP.mult)
                inc(i, "gp")
            qg_c[cc] = cnt["gp"]

            if rl_out is not None:
                src, dstslot, rsems = rl_out
                par = rsems[cc % 2]

                @blk.sync
                def _(s):
                    s.wait_ge(sems["gp"], qg_c[cc])
                    if cc >= 2:
                        s.wait_ge(sems[par], rl_pend[cc - 2])
                    inc(s.dma_start(bfv(dstslot, cc * 1024, 1024),
                                    src[:, cc * 1024:(cc + 1) * 1024]), par, 16)
                rl_pend[cc] = cnt[par]

            @blk.tensor
            def _(t):
                t.wait_ge(sems["gp"], qg_c[cc])
                t.wait_ge(sems["dv"], cnt["dv"])
                for tt in range(2):
                    for h2 in (0, 1):
                        i = t.matmul(PSF[tt][h2 * 64:h2 * 64 + 64, 0:512],
                                     bfv(SBs, B_SIB + cc * 64, 64, p0=h2 * 64, parts=64),
                                     bfv(SBs, B_QG + tt * 512, 512, p0=h2 * 64, parts=64),
                                     start=True, stop=True,
                                     tile_position=(h2 * 64, h2 * 64))
                inc(i, "pe")
            qm_c = cnt["pe"]

            @VEC
            def _(v):
                v.wait_ge(sems["pe"], qm_c)
                if cc == 0 and ysub is not None:
                    v.wait_ge(sems[ysub[0]], ysub[1])
                for tt in range(2):
                    yv = ch(yslot, cc, tt * 512, tt * 512 + 512)
                    inc(v.tensor_tensor(yv, yv, PSF[tt][:, 0:512], OP.subtract), "dv")
        return rl_pend, cnt["pe"], cnt["dv"]

    def phase_S6(ti, yslot, gslot, first, corr_pe=0, corr_dv=0, g_rl=None):
        last = {}
        if first:
            @GPS
            def _(gp):
                gp.wait_ge(sems["pe"], corr_pe)
                inc(gp.dma_start(bfv(SBs, B_LNGR, 2048, parts=1), dram["lng_r"]), "xw", 16)
                i = gp.tensor_copy(glbt.ap().rearrange("p (c two) -> p c two", two=2)[:, :, 0:1],
                                   vecs["lng_c"].ap().unsqueeze(2))
                inc(i, "gp")
                i = gp.tensor_copy(glbt.ap().rearrange("p (c two) -> p c two", two=2)[:, :, 1:2],
                                   vecs["lnb_c"].ap().unsqueeze(2))
                inc(i, "gp")
            lng_xw = cnt["xw"]
            glbt_gp = cnt["gp"]
        else:
            barrier()
            lng_xw = glbt_gp = 0

        ac0 = cnt["ac"]
        ysqAc = {}
        sqAc = {}
        for tt in range(2):
            for cc in range(16):
                ysqAc[(tt, cc)] = ac0 + tt * 17 + cc + 1
            sqAc[tt] = ac0 + tt * 17 + 17
        st_pend = {}
        va_pend = {}

        for tt in range(2):
            for cc in range(16):
                @blk.tensor
                def _(t):
                    t.wait_ge(sems["ac"], ysqAc[(tt, cc)])
                    if cc == 0:
                        t.wait_ge(sems["dv"], last.get(("stev", tt - 1), 0))
                        if tt == 0:
                            t.wait_ge(sems["dv"], corr_dv)
                    t.matmul(PSF[4][0:1, 0:512], onc[:],
                             ch(yslot, cc, tt * 512, tt * 512 + 512),
                             start=(cc == 0), stop=(cc == 15))
                    i = t.matmul(PSF[5][0:1, 0:512], onc[:],
                                 bfv(SBs, [B_YSQ, B_YSQ2][cc % 2], 512),
                                 start=(cc == 0), stop=(cc == 15))
                    inc(i, "pe")
                st_pend[(tt, cc)] = cnt["pe"]
            st_pe = cnt["pe"]

            @VEC
            def _(v):
                v.wait_ge(sems["pe"], st_pe)
                mu = f32v(SBs, MUF, 512, parts=1)
                va = f32v(SBs, VAF, 512, parts=1)
                rs = f32v(SBs, RSF, 512, parts=1)
                inc(v.tensor_scalar(mu, PSF[4][0:1, 0:512], 1.0 / 2048, None, OP.mult), "dv")
                inc(v.tensor_scalar(va, PSF[5][0:1, 0:512], 1.0 / 2048, None, OP.mult), "dv")
                inc(v.tensor_tensor(rs, mu, mu, OP.mult), "dv")
                inc(v.tensor_tensor(va, va, rs, OP.subtract), "dv")
            va_pend[tt] = cnt["dv"]

            @VEC
            def _(v):
                v.wait_ge(sems["ac"], sqAc[tt])
                rs = f32v(SBs, RSF, 512, parts=1)
                nm = f32v(SBs, NMF, 512, parts=1)
                rh = bfv(SBs, RHB + tt * 512, 512, parts=1)
                rl = bfv(SBs, RLB + tt * 512, 512, parts=1)
                nh = bfv(SBs, NHB + tt * 512, 512, parts=1)
                nl = bfv(SBs, NLB + tt * 512, 512, parts=1)
                inc(v.reciprocal(rs, rs), "dv")
                inc(v.tensor_copy(rh, rs), "dv")
                inc(v.tensor_tensor(rl, rs, rh, OP.subtract), "dv")
                inc(v.tensor_tensor(nm, f32v(SBs, MUF, 512, parts=1), rs, OP.mult), "dv")
                inc(v.tensor_scalar(nm, nm, -1.0, None, OP.mult), "dv")
                inc(v.tensor_copy(nh, nm), "dv")
                inc(v.tensor_tensor(nl, nm, nh, OP.subtract), "dv")
            tst_c = cnt["dv"]
            last[("stev", tt)] = tst_c

            for cc in range(16):
                b1 = 2 + (cc % 2)
                @blk.tensor
                def _(t):
                    t.wait_ge(sems["dv"], tst_c)
                    t.wait_ge(sems["dv"], last.get(("yh", tt, cc - 2), 0))
                    if tt == 0 and cc == 0:
                        t.wait_ge(sems["xw"], lng_xw)
                    t.matmul(PSF[b1][:, 0:512], bfv(SBs, B_LNGR + cc * 128, 128, parts=1),
                             bfv(SBs, RHB + tt * 512, 512, parts=1), start=True, stop=False)
                    i = t.matmul(PSF[b1][:, 0:512], bfv(SBs, B_LNGR + cc * 128, 128, parts=1),
                                 bfv(SBs, RLB + tt * 512, 512, parts=1), start=False, stop=True)
                    inc(i, "pe")
                b1_c = cnt["pe"]

                @VEC
                def _(v):
                    v.wait_ge(sems["pe"], b1_c)
                    yv = ch(yslot, cc, tt * 512, tt * 512 + 512)
                    inc(v.tensor_tensor(yv, yv, PSF[b1][:, 0:512], OP.mult), "dv")
                last[("yh", tt, cc)] = cnt["dv"]
        yh_dv = cnt["dv"]

        @blk.scalar
        def _(sc):
            sc.wait_ge(sems["dv"], corr_dv)
            for tt in range(2):
                for cc in range(16):
                    if cc >= 2:
                        sc.wait_ge(sems["pe"], st_pend[(tt, cc - 2)])
                    elif tt == 1 and cc < 2:
                        sc.wait_ge(sems["pe"], st_pend[(0, 14 + cc)])
                    sc.wait_ge(sems["ac"], cnt["ac"])
                    i = sc.activation(bfv(SBs, [B_YSQ, B_YSQ2][cc % 2], 512),
                                      ch(yslot, cc, tt * 512, tt * 512 + 512), AF.Square)
                    inc(i, "ac")
                    assert cnt["ac"] == ysqAc[(tt, cc)]
                sc.wait_ge(sems["dv"], va_pend[tt])
                i = sc.activation(f32v(SBs, RSF, 512, parts=1), f32v(SBs, VAF, 512, parts=1),
                                  AF.Sqrt, bias=epsc[0:1, 0:1], scale=1.0)
                inc(i, "ac")
                assert cnt["ac"] == sqAc[tt]

        for cc in range(16):
            wb = [D_W0, D_W1][cc % 2]

            @blk.sync
            def _(s):
                s.wait_ge(sems["dv"], yh_dv)
                if cc >= 2:
                    s.wait_ge(sems["pe"], last.get(("ow", cc - 2), 0))
                inc(s.dma_start(bfv(SD, wb, 2048), conv_slab(SLAB[("Wo", cc)])),
                    ["wa", "wb"][cc % 2], 16)
            wo_c = cnt[["wa", "wb"][cc % 2]]

            if first:
                @blk.tensor
                def _(t):
                    t.wait_ge(sems[["wa", "wb"][cc % 2]], wo_c)
                    t.wait_ge(sems["dv"], last.get(("gbe", cc - 1), 0))
                    if cc == 0:
                        t.wait_ge(sems["gp"], glbt_gp)
                    for kc in range(16):
                        t.matmul(PSF[5][0:1, 0:128],
                                 glbt[:, 2 * kc:2 * kc + 1],
                                 bfv(SD, wb + kc * 128, 128),
                                 start=(kc == 0), stop=(kc == 15))
                    for kc in range(16):
                        i = t.matmul(PSF[5][0:1, 128:256],
                                     glbt[:, 2 * kc + 1:2 * kc + 2],
                                     bfv(SD, wb + kc * 128, 128),
                                     start=(kc == 0), stop=(kc == 15))
                    inc(i, "pe")
                gb_pe = cnt["pe"]

                @VEC
                def _(v):
                    v.wait_ge(sems["pe"], gb_pe)
                    inc(v.tensor_copy(bfv(SBs, B_GROW + cc * 128, 128, parts=1),
                                      PSF[5][0:1, 0:128]), "dv")
                    inc(v.tensor_copy(bfv(SBs, B_BROW + cc * 128, 128, parts=1),
                                      PSF[5][0:1, 128:256]), "dv")
                last[("gbe", cc)] = cnt["dv"]
            gbe_c = last.get(("gbe", cc), 0)

            for tt in range(2):
                pb = (cc % 2) * 2 + tt
                @blk.tensor
                def _(t):
                    t.wait_ge(sems[["wa", "wb"][cc % 2]], wo_c)
                    t.wait_ge(sems["dv"], gbe_c)
                    t.wait_ge(sems["dv"], last.get(("sg", tt, cc - 2), 0))
                    for kc in range(16):
                        t.matmul(PSF[pb][:, 0:512], bfv(SD, wb + kc * 128, 128),
                                 ch(yslot, kc, tt * 512, tt * 512 + 512),
                                 start=(kc == 0), stop=False)
                    t.matmul(PSF[pb][:, 0:512], bfv(SBs, B_GROW + cc * 128, 128, parts=1),
                             bfv(SBs, NHB + tt * 512, 512, parts=1), start=False, stop=False)
                    t.matmul(PSF[pb][:, 0:512], bfv(SBs, B_GROW + cc * 128, 128, parts=1),
                             bfv(SBs, NLB + tt * 512, 512, parts=1), start=False, stop=False)
                    i = t.matmul(PSF[pb][:, 0:512], bfv(SBs, B_BROW + cc * 128, 128, parts=1),
                                 onr[:], start=False, stop=True)
                    inc(i, "pe")
                s_pe = cnt["pe"]
                if tt == 1:
                    last[("ow", cc)] = s_pe

                @VEC
                def _(v):
                    v.wait_ge(sems["pe"], s_pe)
                    if g_rl is not None and tt == 0:
                        sm, val = g_rl[cc]
                        v.wait_ge(sems[sm], val)
                    gv = ch(gslot, cc, tt * 512, tt * 512 + 512)
                    inc(v.tensor_tensor(gv, PSF[pb][:, 0:512], gv, OP.mult), "dv")
                last[("sg", tt, cc)] = cnt["dv"]
        sg_dv = cnt["dv"]

        prev = {}
        for sl in range(NSLAB):
            row0 = ti * TT + sl * 128
            iob = [D_IO0F, D_IO1F][sl % 2]
            xio = f32v(SD, iob, 2048)

            lsem = ["ia", "ib"][sl % 2]
            ssem = ["sa", "sb"][sl % 2]

            @blk.sync
            def _(s):
                s.wait_ge(sems[ssem], prev.get(("store", sl % 2), 0))
                inc(s.dma_start(xio, dram["xin"][row0:row0 + 128, :]), lsem, 16)
            xl_c = cnt[lsem]

            for half in range(2):
                bb = half
                @blk.tensor
                def _(t):
                    t.wait_ge(sems["dv"], sg_dv)
                    t.wait_ge(sems["dv"], prev.get(("fin", sl - 1, bb), 0))
                    for j in range(8):
                        cc = half * 8 + j
                        i = t.matmul(PSB[bb][:, j * 128:(j + 1) * 128],
                                     ch(gslot, cc, sl * 128, sl * 128 + 128), idn[:],
                                     start=True, stop=True, is_transpose=True)
                    inc(i, "pe")
                tp_c = cnt["pe"]

                @VEC
                def _(v):
                    v.wait_ge(sems["pe"], tp_c)
                    v.wait_ge(sems[lsem], xl_c)
                    i = v.tensor_tensor(xio[:, half * 1024:half * 1024 + 1024],
                                        xio[:, half * 1024:half * 1024 + 1024],
                                        PSB[bb][:, 0:1024], OP.add)
                    inc(i, "dv")
                prev[("fin", sl, bb)] = cnt["dv"]
            fin_c = cnt["dv"]

            @blk.sync
            def _(s):
                s.wait_ge(sems["dv"], fin_c)
                inc(s.dma_start(out[row0:row0 + 128, :], xio), ssem, 16)
            prev[("store", sl % 2)] = cnt[ssem]
        barrier()

    SLAB = {}
    for si, nm in enumerate(["Wq", "Wk", "Wv", "Wo", "Wf"]):
        for cc in range(16):
            SLAB[(nm, cc)] = si * 16 + cc
    SLAB[("Wa", 0)] = 80
    SLAB[("Wa", 1)] = 81

    def conv_slab(j):
        return conv_out[j // 11, j % 11]

    conv_cc = [0]

    def phase_conv():
        lastout = {}
        for i in range(11):
            stg = bfv(SC, (i % 2) * 2048, 2048)
            vsem = ["va", "vb"][i % 2]
            ssem = ["sa", "sb"][i % 2]

            @GPS
            def _(gp):
                if i >= 2:
                    gp.wait_ge(sems[ssem], lastout[i - 2])
                inc(gp.dma_start(stg, dram["Wsrc"][i]), vsem, 16)
            in_c = cnt[vsem]

            @blk.sync
            def _(s):
                s.wait_ge(sems[vsem], in_c)
                inc(s.dma_start(conv_in[i], stg), ssem, 16)
            lastout[i] = cnt[ssem]

        @GPS
        def _(gp):
            gp.wait_ge(sems["sa"], cnt["sa"])
            gp.wait_ge(sems["sb"], cnt["sb"])
            i = gp.collective_compute("AllGather", OP.bypass,
                                      replica_groups=[list(range(8))],
                                      ins=[conv_in], outs=[conv_out])
            inc(i, "cc")
        conv_cc[0] = cnt["cc"]

    step_n = [0]

    def step():
        step_n[0] += 1
        return step_n[0] <= LIMIT

    for ti in range(2):
        if step():
            phase_S1(ti)
        if ti == 0:
            phase_conv()
        dump("d_xn%d" % ti, SA[:, :])
        if step():
            phase_S2(ti)
        dump("d_expc%d" % ti, SC[:, :])
        dump("d_f1%d" % ti, SD[:, :])
        dump("d_gam%d" % ti, Gam[:, :])
        dump("d_goff%d" % ti, goffT[ti][:, :])
        if step():
            phase_S3(ti, "q")
        dump("d_q%d" % ti, SC[:, :])
        if step():
            phase_S3(ti, "k")
        dump("d_k%d" % ti, SD[:, :])
        if step():
            phase_S3(ti, "v")
        dump("d_v%d" % ti, SE[:, :])
        if step():
            phase_S4(ti)
        dump("d_g%d" % ti, SF[:, :])
        if ti == 0:
            @GPS
            def _(gp):
                inc(gp.tensor_copy(xnb[:], v3(SA)[:, :, 1023:1024].squeeze(2)), "gp")
            barrier()
            @blk.sync
            def _(s):
                inc(s.dma_start(sp_q, SC[:, :]), "xs", 16)
                inc(s.dma_start(sp_g, SF[:, :]), "xs", 16)
        if step():
            phase_S5(ti)
        dump("d_yr%d" % ti, SA[:, :])
        if ti == 0:
            @blk.sync
            def _(s):
                inc(s.dma_start(sp_y, SA[:, :]), "xs", 16)
            barrier()
    if step():
        phase_expg()
    spy_rl = 0
    if step():
        spy_rl = phase_exchange()
    c1 = c0 = None
    if step():
        c1 = phase_corr(1, SA, SC, rl_out=(sp_q, SC, ("ia", "ib")))
    if step():
        rl_in0 = {cc: (("ia", "ib")[cc % 2], c1[0][cc]) for cc in range(16)}
        c0 = phase_corr(0, SE, SC, rl_in=rl_in0,
                        rl_out=(sp_g, SC, ("sa", "sb")), ysub=("va", spy_rl))
    if step():
        g_rl0 = {cc: (("sa", "sb")[cc % 2], c0[0][cc]) for cc in range(16)}
        phase_S6(0, SE, SC, first=True, corr_pe=c0[1], corr_dv=c0[2], g_rl=g_rl0)
    if step():
        phase_S6(1, SA, SF, first=False)

    @blk.sync
    def _(s):
        s.wait_ge(sems["sa"], cnt["sa"])
        s.wait_ge(sems["sb"], cnt["sb"])
        s.wait_ge(sems["st"], cnt["st"])

    blkcm.__exit__(None, None, None)
    for cm in reversed(ctxs):
        cm.__exit__(None, None, None)
    for cm in reversed(semcms):
        cm.__exit__(None, None, None)
    return nc


_cache = {}


def kernel(**ins):
    maps = build_inputs(ins)
    key = "prog"
    if key not in _cache:
        _cache[key] = build_program(debug=DEBUG)
    nc = _cache[key]
    res = run_bass_kernel_spmd(nc, maps, core_ids=list(range(8)))
    outp = np.empty((4, 4096, 2048), dtype=np.float32)
    for c in range(8):
        b, hf = c // 2, c % 2
        outp[b, hf * TC:(hf + 1) * TC] = res.results[c]["out"]
    return outp


# version 19
# speedup vs baseline: 1.0211x; 1.0211x over previous
import math
import numpy as np
import ml_dtypes
import concourse.bass as bass
import concourse.mybir as mybir
from concourse.bass_utils import run_bass_kernel_spmd
from concourse.alu_op_type import AluOpType as OP

F32 = mybir.dt.float32
BF16 = mybir.dt.bfloat16
AF = mybir.ActivationFunctionType
AX = mybir.AxisListType
bfloat16 = ml_dtypes.bfloat16

C = 2048
TC = 2048
TT = 1024
L = 64
NCH = 16
NSLAB = 8
LN8 = math.log(1.0 / 8.0)

DEBUG = []
LIMIT = 10**9
S1N = 8
S1CUT = 9
S5CUT = 9


def _prep_proj(M):
    return np.ascontiguousarray(
        M.reshape(16, 128, 16, 128).transpose(2, 1, 0, 3).reshape(16, 128, 2048))


def _colfm(v):
    return np.ascontiguousarray(v.reshape(16, 128).T)


def build_inputs(ins):
    x = np.asarray(ins["L0_attn_x"], dtype=np.float32)
    g = {}
    g["Wq"] = _prep_proj(np.asarray(ins["L0_attn_q_w"], np.float32))
    g["Wk"] = _prep_proj(np.asarray(ins["L0_attn_k_w"], np.float32))
    g["Wv"] = _prep_proj(np.asarray(ins["L0_attn_v_w"], np.float32))
    g["Wo"] = _prep_proj(np.asarray(ins["L0_attn_o_w"], np.float32))
    g["Wf"] = _prep_proj(np.ascontiguousarray(
        np.asarray(ins["L0_ffn_W"], np.float32).T))
    A = np.asarray(ins["L0_ffn_A"], np.float32)
    Wa = np.ascontiguousarray(A.transpose(1, 0, 2).reshape(256, 2048).T)
    g["Wa"] = np.ascontiguousarray(
        Wa.reshape(16, 128, 2, 128).transpose(2, 1, 0, 3).reshape(2, 128, 2048))
    Bm = np.asarray(ins["L0_ffn_B"], np.float32)
    Wb = np.ascontiguousarray(Bm.transpose(2, 0, 1).reshape(256, 2048))
    g["Wb"] = np.ascontiguousarray(
        Wb.reshape(2, 128, 16, 128).transpose(2, 1, 0, 3).reshape(16, 128, 256))
    g["Wmm"] = np.ascontiguousarray(
        np.asarray(ins["L0_attn_maa_w1"], np.float32)
        .reshape(16, 128, 128).transpose(1, 0, 2).reshape(128, 2048))
    g["Wm2"] = np.ascontiguousarray(
        np.asarray(ins["L0_attn_maa_w2"], np.float32).reshape(128, 16, 128))
    g["Wt1"] = np.ascontiguousarray(
        np.asarray(ins["L0_attn_td_w1"], np.float32)
        .reshape(16, 128, 64).transpose(1, 0, 2).reshape(128, 1024))
    g["Wt2"] = np.ascontiguousarray(np.asarray(ins["L0_attn_td_w2"], np.float32))
    g["Wr1"] = np.ascontiguousarray(
        np.asarray(ins["L0_ffn_rw1"], np.float32).T
        .reshape(16, 128, 64).transpose(1, 0, 2).reshape(128, 1024))
    g["Wr2"] = np.ascontiguousarray(np.asarray(ins["L0_ffn_rw2"], np.float32)[:16].T)
    for nm, key in [("maax_c", "L0_attn_maa_x"), ("maar_c", "L0_attn_maa_r"),
                    ("maak_c", "L0_attn_maa_k"), ("maav_c", "L0_attn_maa_v"),
                    ("maaw_c", "L0_attn_maa_w"), ("qb_c", "L0_attn_q_b"),
                    ("kb_c", "L0_attn_k_b"), ("vb_c", "L0_attn_v_b"),
                    ("tdec_c", "L0_attn_time_decay"), ("lng_c", "L0_attn_ln_g"),
                    ("lnb_c", "L0_attn_ln_b"), ("fb_c", "L0_ffn_bias")]:
        g[nm] = _colfm(np.asarray(ins[key], np.float32))
    g["rb1_c"] = np.ascontiguousarray(np.asarray(ins["L0_ffn_rb1"], np.float32)[:, None])
    g["rb2_r"] = np.ascontiguousarray(np.asarray(ins["L0_ffn_rb2"], np.float32)[:16][None])
    g["gat_r"] = np.ascontiguousarray(np.asarray(ins["L0_ffn_gates"], np.float32)[None])
    g["lng_r"] = np.ascontiguousarray(np.asarray(ins["L0_attn_ln_g"], np.float32)[None])
    g["ident"] = np.eye(128, dtype=bfloat16)
    g["trimask"] = (np.arange(64)[None, :] >= (np.arange(128) % 64)[:, None]).astype(bfloat16)
    g["ones_r"] = np.ones((1, 512), dtype=bfloat16)
    g["ones_c"] = np.ones((128, 1), dtype=bfloat16)
    g["repR"] = (np.arange(128)[None, :] % 16 == np.arange(16)[:, None]).astype(bfloat16)
    slabs = []
    for nm in ["Wq", "Wk", "Wv", "Wo", "Wf"]:
        for cc in range(16):
            slabs.append(g[nm][cc])
    slabs.append(g["Wa"][0]); slabs.append(g["Wa"][1])
    for nm in ["Wq", "Wk", "Wv", "Wo", "Wf", "Wa"]:
        del g[nm]
    maps = []
    for c in range(8):
        b, hf = c // 2, c % 2
        t0 = hf * TC
        m = dict(g)
        m["Wsrc"] = np.ascontiguousarray(np.stack(
            [slabs[min(11 * c + i, 81)] for i in range(11)]))
        m["xin"] = np.ascontiguousarray(x[b, t0:t0 + TC])
        m["xprev"] = (np.ascontiguousarray(x[b, t0 - 1:t0]) if t0 > 0
                      else np.zeros((1, C), np.float32))
        m["pmask"] = np.full((128, 1), 1.0 if t0 > 0 else 0.0, np.float32)
        sm = np.zeros((128, 8), np.float32)
        if hf == 1:
            sm[:, c - 1] = 1.0
        m["smask"] = sm
        maps.append(m)
    return maps


def build_program(debug=()):
    nc = bass.Bass("TRN2", target_bir_lowering=False)
    cnt = {}
    sems = {}

    semcms = []

    def mksem(name):
        cm = nc.semaphore(name)
        sems[name] = cm.__enter__()
        semcms.append(cm)
        cnt[name] = 0

    def inc(instr, name, n=1):
        instr.then_inc(sems[name], n)
        cnt[name] += n
        return cnt[name]

    def seminc(eng, name, n=1):
        eng.sem_inc(sems[name], n)
        cnt[name] += n
        return cnt[name]

    dram = {}

    def din(name, shape, dt=F32):
        dram[name] = nc.dram_tensor(name, shape, dt, kind="ExternalInput").ap()

    din("xin", [TC, C]); din("xprev", [1, C]); din("pmask", [128, 1]); din("smask", [128, 8])
    din("Wsrc", [11, 128, 2048])
    din("Wb", [16, 128, 256])
    din("Wmm", [128, 2048]); din("Wm2", [128, 16, 128])
    din("Wt1", [128, 1024]); din("Wt2", [64, 2048])
    din("Wr1", [128, 1024]); din("Wr2", [64, 16])
    VECN = ["maax_c", "maar_c", "maak_c", "maav_c", "maaw_c", "qb_c", "kb_c",
            "vb_c", "tdec_c", "lng_c", "lnb_c", "fb_c"]
    for nm in VECN:
        din(nm, [128, 16])
    din("rb1_c", [64, 1]); din("rb2_r", [1, 16]); din("gat_r", [1, 16]); din("lng_r", [1, 2048])
    din("ident", [128, 128], BF16); din("trimask", [128, 64], BF16)
    din("ones_r", [1, 512], BF16); din("ones_c", [128, 1], BF16); din("repR", [16, 128], BF16)
    out = nc.dram_tensor("out", [TC, C], F32, kind="ExternalOutput").ap()
    dbgt = {}
    for nm, shape, dts in debug:
        dbgt[nm] = nc.dram_tensor(nm, shape, {"f32": F32, "bf16": BF16}[dts],
                                  kind="ExternalOutput").ap()
    sp_y = nc.dram_tensor("sp_y", [128, 16384], BF16, kind="Internal").ap()
    sp_g = nc.dram_tensor("sp_g", [128, 16384], BF16, kind="Internal").ap()
    sp_q = nc.dram_tensor("sp_q", [128, 16384], BF16, kind="Internal").ap()
    conv_in = nc.dram_tensor("conv_in", [11, 128, 2048], BF16, kind="Internal").ap()
    conv_out = nc.dram_tensor("conv_out", [8, 11, 128, 2048], BF16, kind="Internal",
                              addr_space="Shared").ap()
    cc_in = nc.dram_tensor("cc_in", [128, 1024], F32, kind="Internal").ap()
    cc_out = nc.dram_tensor("cc_out", [8, 128, 1024], F32, kind="Internal",
                            addr_space="Shared").ap()
    ctxs = []

    def sbuf(name, shape, dt):
        cm = nc.sbuf_tensor(name, shape, dt)
        h = cm.__enter__()
        ctxs.append(cm)
        return h

    def psum(name, shape, dt):
        cm = nc.psum_tensor(name, shape, dt)
        h = cm.__enter__()
        ctxs.append(cm)
        return h

    SA = sbuf("SA", [128, 16384], BF16)
    SBs = sbuf("SBs", [128, 16384], BF16)
    SC = sbuf("SC", [128, 16384], BF16)
    SD = sbuf("SD", [128, 16384], BF16)
    SE = sbuf("SE", [128, 16384], BF16)
    SF = sbuf("SF", [128, 16384], BF16)
    S_all = sbuf("S_state", [128, 1024], F32)
    Gam = sbuf("Gam", [128, 256], F32)
    goffT = [sbuf("goffA", [128, 256], F32), sbuf("goffB", [128, 256], F32)]
    gcar = sbuf("gcar", [128, 16], F32)
    gb0s = sbuf("gb0s", [128, 32], F32)
    xnb = sbuf("xnb", [128, 16], BF16)
    st1 = sbuf("st1", [128, 16], F32)
    vecs = {nm: sbuf("v_" + nm, [128, 16], F32) for nm in VECN}
    pmsk = sbuf("v_pmask", [128, 1], F32)
    smsk = sbuf("v_smask", [128, 8], F32)
    rb1 = sbuf("rb1s", [64, 1], F32)
    idn = sbuf("idn", [128, 128], BF16)
    tri = sbuf("tris", [128, 64], BF16)
    onr = sbuf("onrs", [1, 512], BF16)
    onc = sbuf("oncs", [128, 1], BF16)
    repR = sbuf("repRs", [16, 128], BF16)
    rb2r = sbuf("rb2rs", [1, 16], BF16)
    gatr = sbuf("gatrs", [1, 16], BF16)
    glbt = sbuf("glbt", [128, 32], BF16)
    epsc = sbuf("epsc", [128, 2], F32)
    wr2s = sbuf("wr2s", [64, 16], BF16)

    PSF = [psum(f"psf{i}", [128, 512], F32) for i in range(6)]
    PSB = [psum(f"psb{i}", [128, 1024], BF16) for i in range(2)]

    for s in ["xw", "xs", "pe", "dv", "ac", "gp", "st", "cc", "sy",
              "wa", "wb", "ia", "ib", "sa", "sb", "va", "vb", "ic", "id"]:
        mksem(s)

    blkcm = nc.Block()
    blk = blkcm.__enter__()

    def _safe(eng, sem):
        class P:
            def __getattr__(self, name):
                f = getattr(eng, name)
                if name in ("wait_ge",) or not callable(f):
                    return f

                def g(*a, **k):
                    eng.wait_ge(sems[sem], cnt[sem])
                    return f(*a, **k)
                return g
        return P()

    def VEC(f):
        @blk.vector
        def _(v):
            f(_safe(v, "dv"))

    def SCA(f):
        @blk.scalar
        def _(sc):
            f(_safe(sc, "ac"))

    def GPS(f):
        @blk.gpsimd
        def _(gp):
            f(_safe(gp, "gp"))

    def v3(slot):
        return slot.ap().rearrange("p (c t) -> p c t", c=16)

    def ch(slot, cc, t0=0, t1=TT):
        return slot[:, cc * 1024 + t0: cc * 1024 + t1]

    def f32v(slot, off, n, p0=0, parts=128):
        return slot.ap().bitcast(F32)[p0:p0 + parts, off: off + n]

    def bfv(slot, off, n, p0=0, parts=128):
        return slot[p0:p0 + parts, off:off + n]

    W0, W1 = 0, 2048
    MMT, TDH, DXC = 4096, 5120, 6144
    XXA, XXB = 7168, 8192
    T2AF, T2BF = 4608, 5632
    WM2 = 13312
    EWF, GCF, CRF, E2F, XPF = 0, 1024, 2048, 3072, 4096
    B_WXA, B_H, B_COF, B_CFF, B_WR2, B_WB, B_CRP = 0, 2048, 3072, 3136, 3648, 3712, 3968
    B_WB2 = 4480
    B_WR1 = 13312
    B_WS0, B_WS1 = 8192, 10240
    LTMF, M1F, M2F, I1F, I2F, LMF, P1F, PDF, P2F, C1F, C2F = \
        6144, 6208, 6212, 6216, 6280, 6344, 6408, 6412, 6416, 6420, 6484
    B_VT, B_KT, B_KF, B_AM = 0, 2048, 4096, 6144
    TS5F = 4096
    B_SIB, B_QG = 0, 1024
    GB0F, GB1F = 1024, 2048
    EXG16 = 5440
    B_LNGR, B_GROW, B_BROW = 0, 2048, 4096
    B_YSQ = 6272
    B_YSQ2 = 15360
    MUF, VAF, RSF, NMF = 3392, 3904, 4416, 4928
    RHB, RLB, NHB, NLB = 10880, 11904, 12928, 13952
    D_XTM = 12288
    D_W0, D_W1 = 0, 2048
    D_IO0F, D_IO1F = 2048, 4096
    S_bf = bfv(SBs, 10240, 1024)
    S_inv = f32v(SBs, 3200, 1024)

    marks = {}

    def barrier():
        @GPS
        def _(gp):
            gp.wait_ge(sems["xw"], cnt["xw"])
            for k in ["va", "vb"]:
                gp.wait_ge(sems[k], cnt[k])
            seminc(gp, "gp")

        @blk.sync
        def _(s):
            s.wait_ge(sems["xs"], cnt["xs"])
            for k in ["ia", "ib", "ic", "id", "sa", "sb", "st", "wa", "wb"]:
                s.wait_ge(sems[k], cnt[k])
            seminc(s, "sy")

        m = {k: cnt[k] for k in ["pe", "dv", "ac", "gp", "sy", "cc"]}

        @blk.tensor
        def _(t):
            for k in ["dv", "ac", "gp", "sy"]:
                t.wait_ge(sems[k], m[k])

        @VEC
        def _(v):
            for k in ["pe", "ac", "gp", "sy"]:
                v.wait_ge(sems[k], m[k])

        @SCA
        def _(sc):
            for k in ["pe", "dv", "gp", "sy"]:
                sc.wait_ge(sems[k], m[k])

        @GPS
        def _(gp):
            for k in ["pe", "dv", "ac", "sy"]:
                gp.wait_ge(sems[k], m[k])

        @blk.sync
        def _(s):
            for k in ["pe", "dv", "ac", "gp"]:
                s.wait_ge(sems[k], m[k])
        return m

    def dump(name, src_ap):
        if name in dbgt:
            barrier()

            @blk.sync
            def _(s):
                inc(s.dma_start(dbgt[name], src_ap), "st", 16)
                s.wait_ge(sems["st"], cnt["st"])
            barrier()

    @blk.sync
    def _(s):
        for nm in VECN:
            inc(s.dma_start(vecs[nm][:], dram[nm]), "xs", 16)
        inc(s.dma_start(pmsk[:], dram["pmask"]), "xs", 16)
        inc(s.dma_start(smsk[:], dram["smask"]), "xs", 16)
        inc(s.dma_start(rb1[:], dram["rb1_c"]), "xs", 16)
        inc(s.dma_start(idn[:], dram["ident"]), "xs", 16)
        inc(s.dma_start(tri[:], dram["trimask"]), "xs", 16)
        inc(s.dma_start(onr[:], dram["ones_r"]), "xs", 16)
        inc(s.dma_start(onc[:], dram["ones_c"]), "xs", 16)
        inc(s.dma_start(repR[:], dram["repR"]), "xs", 16)

    @GPS
    def _(gp):
        inc(gp.dma_start(rb2r[:], dram["rb2_r"]), "xw", 16)
        inc(gp.dma_start(gatr[:], dram["gat_r"]), "xw", 16)
        inc(gp.dma_start(wr2s[:], dram["Wr2"]), "xw", 16)
        inc(gp.dma_start(bfv(SF, W0, 2048), dram["Wmm"]), "xw", 16)
        inc(gp.dma_start(bfv(SF, W1, 1024), dram["Wt1"]), "xw", 16)
        inc(gp.dma_start(bfv(SF, WM2, 2048),
                         dram["Wm2"].rearrange("p c k -> p (c k)")), "xw", 16)
        gp.memset(epsc[:, 0:1], 1e-5)
        gp.memset(epsc[:, 1:2], LN8)
        gp.memset(S_all[:], 0.0)
        inc(gp.memset(gcar[:], 0.0), "gp")
    barrier()

    @blk.sync
    def _(s):
        inc(s.dma_start(f32v(SD, 0, 2048, parts=1), dram["xprev"]), "xs", 16)
        s.wait_ge(sems["xs"], cnt["xs"])
        seminc(s, "sy")
    c1 = cnt["sy"]

    @VEC
    def _(v):
        v.wait_ge(sems["sy"], c1)
        inc(v.reduce_sum(st1[0:1, 0:1], f32v(SD, 0, 2048, parts=1), axis=AX.X), "dv")
        inc(v.tensor_tensor(f32v(SD, 2048, 2048, parts=1), f32v(SD, 0, 2048, parts=1),
                            f32v(SD, 0, 2048, parts=1), OP.mult), "dv")
        inc(v.reduce_sum(st1[0:1, 1:2], f32v(SD, 2048, 2048, parts=1), axis=AX.X), "dv")
        inc(v.tensor_scalar(st1[0:1, 2:3], st1[0:1, 0:1], 1.0 / 2048, None, OP.mult), "dv")
        inc(v.tensor_scalar(st1[0:1, 3:4], st1[0:1, 1:2], 1.0 / 2048, None, OP.mult), "dv")
        inc(v.tensor_tensor(st1[0:1, 4:5], st1[0:1, 2:3], st1[0:1, 2:3], OP.mult), "dv")
        inc(v.tensor_tensor(st1[0:1, 3:4], st1[0:1, 3:4], st1[0:1, 4:5], OP.subtract), "dv")
    c2 = cnt["dv"]

    @SCA
    def _(sc):
        sc.wait_ge(sems["dv"], c2)
        inc(sc.activation(st1[0:1, 5:6], st1[0:1, 3:4], AF.Sqrt,
                          bias=epsc[0:1, 0:1], scale=1.0), "ac")
    c3 = cnt["ac"]

    @VEC
    def _(v):
        v.wait_ge(sems["ac"], c3)
        inc(v.reciprocal(st1[0:1, 6:7], st1[0:1, 5:6]), "dv")
        inc(v.tensor_scalar(bfv(SD, 8192, 2048, parts=1), f32v(SD, 0, 2048, parts=1),
                            st1[0:1, 2:3], st1[0:1, 6:7], OP.subtract, OP.mult), "dv")
    c4 = cnt["dv"]

    @blk.tensor
    def _(t):
        t.wait_ge(sems["dv"], c4)
        for cc2 in range(16):
            i = t.matmul(PSB[0][:, 2 * cc2:2 * cc2 + 1], bfv(SD, 8192 + cc2 * 128, 128, parts=1),
                         idn[0:1, 0:1], start=True, stop=True, is_transpose=True)
        inc(i, "pe")
    c5 = cnt["pe"]

    @VEC
    def _(v):
        v.wait_ge(sems["pe"], c5)
        inc(v.tensor_scalar(xnb[:], PSB[0][:, 0:32:2], pmsk[:, 0:1], None, OP.mult), "dv")
    barrier()

    ev_c_g = {}

    def phase_S1(ti):
        NS = min(NSLAB, S1N)
        ac0 = cnt["ac"]
        sqAc = {k: ac0 + 2 * k + 1 for k in range(NS)}
        evAc = {k: ac0 + 2 * k + 2 for k in range(NS)}
        ld_c, sq_c, red_c, va_c, xtm_c, tp_c, ev_c = {}, {}, {}, {}, {}, {}, {}

        def bufs(sl):
            return (sl % 4, sl % 2, ["ia", "ib", "ic", "id"][sl % 4])

        for sl in range(NS):
            row0 = ti * TT + sl * 128
            buf, sb2, lsem = bufs(sl)
            xld = f32v(SD, buf * 2048, 2048)
            xsq = f32v(SE, sb2 * 2048, 2048)

            @blk.sync
            def _(s):
                s.wait_ge(sems["dv"], xtm_c.get(sl - 4, 0))
                inc(s.dma_start(xld, dram["xin"][row0:row0 + 128, :]), lsem, 16)
            ld_c[sl] = cnt[lsem]

            @GPS
            def _(gp):
                gp.wait_ge(sems[lsem], ld_c[sl])
                gp.wait_ge(sems["dv"], red_c.get(sl - 2, 0))
                inc(gp.tensor_mul(xsq, xld, xld), "gp")
            sq_c[sl] = cnt["gp"]

            so = sb2 * 8
            @VEC
            def _(v):
                v.wait_ge(sems[lsem], ld_c[sl])
                v.wait_ge(sems["ac"], sqAc[sl - 2] if sl >= 2 else 0)
                inc(v.reduce_sum(st1[:, so:so + 1], xld, axis=AX.X), "dv")
                v.wait_ge(sems["gp"], sq_c[sl])
                inc(v.reduce_sum(st1[:, so + 1:so + 2], xsq, axis=AX.X), "dv")
            red_c[sl] = cnt["dv"]

            pv = sl - 1
            if pv >= 0:
                emit_stats(pv, ti, va_c, xtm_c, tp_c, sqAc, evAc, bufs)
        emit_stats(NS - 1, ti, va_c, xtm_c, tp_c, sqAc, evAc, bufs)

        @blk.scalar
        def _(sc):
            for k in range(NS):
                sc.wait_ge(sems["dv"], va_c[k])
                so2 = (k % 2) * 8
                i = sc.activation(st1[:, so2 + 5:so2 + 6], st1[:, so2 + 3:so2 + 4],
                                  AF.Sqrt, bias=epsc[:, 0:1], scale=1.0)
                inc(i, "ac")
                assert cnt["ac"] == sqAc[k]
                sc.wait_ge(sems["pe"], tp_c[k])
                dst = v3(SA)[:, 8:16, k * 128:(k + 1) * 128]
                src2 = PSB[1].ap().rearrange("p (j t) -> p j t", j=8)
                i = sc.activation(dst, src2, AF.Identity, scale=1.0)
                inc(i, "ac")
                assert cnt["ac"] == evAc[k]
        barrier()

    def emit_stats(pv, ti, va_c, xtm_c, tp_c, sqAc, evAc, bufs):
        if pv in va_c:
            return
        buf, sb2, lsem = bufs(pv)
        so = sb2 * 8
        xld = f32v(SD, buf * 2048, 2048)
        xtm = bfv(SE, 8192 + sb2 * 2048, 2048)

        @VEC
        def _(v):
            inc(v.tensor_scalar(st1[:, so + 2:so + 3], st1[:, so:so + 1],
                                1.0 / 2048, None, OP.mult), "dv")
            inc(v.tensor_scalar(st1[:, so + 3:so + 4], st1[:, so + 1:so + 2],
                                1.0 / 2048, None, OP.mult), "dv")
            inc(v.tensor_tensor(st1[:, so + 4:so + 5], st1[:, so + 2:so + 3],
                                st1[:, so + 2:so + 3], OP.mult), "dv")
            inc(v.tensor_tensor(st1[:, so + 3:so + 4], st1[:, so + 3:so + 4],
                                st1[:, so + 4:so + 5], OP.subtract), "dv")
        va_c[pv] = cnt["dv"]

        @VEC
        def _(v):
            v.wait_ge(sems["ac"], sqAc[pv])
            inc(v.reciprocal(st1[:, so + 6:so + 7], st1[:, so + 5:so + 6]), "dv")
            if pv >= 2:
                v.wait_ge(sems["pe"], tp_c[pv - 2])
            inc(v.tensor_scalar(xtm, xld, st1[:, so + 2:so + 3],
                                st1[:, so + 6:so + 7], OP.subtract, OP.mult), "dv")
        xtm_c[pv] = cnt["dv"]

        for half in range(2):
            bb = half
            @blk.tensor
            def _(t):
                t.wait_ge(sems["dv"], xtm_c[pv])
                if pv >= 1:
                    if half == 0:
                        t.wait_ge(sems["dv"], ev_c_g.get(pv - 1, 0))
                    else:
                        t.wait_ge(sems["ac"], evAc[pv - 1])
                for j in range(8):
                    cc = half * 8 + j
                    i = t.matmul(PSB[bb][:, j * 128:(j + 1) * 128],
                                 xtm[:, cc * 128:(cc + 1) * 128], idn[:],
                                 start=True, stop=True, is_transpose=True)
                inc(i, "pe")
            if half == 1:
                tp_c[pv] = cnt["pe"]

            if half == 0:
                @VEC
                def _(v):
                    v.wait_ge(sems["pe"], cnt["pe"])
                    dst = v3(SA)[:, 0:8, pv * 128:(pv + 1) * 128]
                    src2 = PSB[0].ap().rearrange("p (j t) -> p j t", j=8)
                    inc(v.tensor_copy(dst, src2), "dv")
                ev_c_g[pv] = cnt["dv"]

    def emit_dx_full(v, cc):
        xn = v3(SA)
        inc(v.tensor_tensor(ch(SE, cc, 0, 1), xnb[:, cc:cc + 1],
                            xn[:, cc, 0:1], OP.subtract), "dv")
        inc(v.tensor_tensor(ch(SE, cc, 1, 1024), xn[:, cc, 0:1023],
                            xn[:, cc, 1:1024], OP.subtract), "dv")

    def phase_S2(ti):
        xn = v3(SA)

        @GPS
        def _(gp):
            if ti > 0:
                inc(gp.dma_start(bfv(SF, W0, 2048), dram["Wmm"]), "xw", 16)
                inc(gp.dma_start(bfv(SF, W1, 1024), dram["Wt1"]), "xw", 16)
                inc(gp.dma_start(bfv(SF, WM2, 2048),
                                 dram["Wm2"].rearrange("p c k -> p (c k)")), "xw", 16)
                gp.wait_ge(sems["xw"], cnt["xw"])
            seminc(gp, "gp")
        wl_c = cnt["gp"]

        pe_after = {}
        for cc in range(16):
            xb = [XXA, XXB][cc % 2]

            @VEC
            def _(v):
                if cc >= 2:
                    v.wait_ge(sems["pe"], pe_after[cc - 2])
                emit_dx_full(v, cc)
                i = v.scalar_tensor_tensor(bfv(SF, xb, 1024), ch(SE, cc),
                                           vecs["maax_c"][:, cc:cc + 1],
                                           xn[:, cc, :], OP.mult, OP.add)
                inc(i, "dv")
            dv_c = cnt["dv"]

            @blk.tensor
            def _(t):
                t.wait_ge(sems["dv"], dv_c)
                if cc == 0:
                    t.wait_ge(sems["gp"], wl_c)
                for tt in range(2):
                    i = t.matmul(PSF[tt][:, 0:512], bfv(SF, W0 + cc * 128, 128),
                                 bfv(SF, xb + tt * 512, 512),
                                 start=(cc == 0), stop=(cc == 15))
                inc(i, "pe")
            pe_after[cc] = cnt["pe"]
        mm_pe = cnt["pe"]

        @SCA
        def _(sc):
            sc.wait_ge(sems["pe"], mm_pe)
            for tt in range(2):
                inc(sc.activation(bfv(SF, MMT + tt * 512, 512), PSF[tt][:, 0:512],
                                  AF.Tanh), "ac")
        barrier()

        @GPS
        def _(gp):
            inc(gp.dma_start(bfv(SF, W0, 2048, parts=64), dram["Wt2"]), "xw", 16)
            gp.wait_ge(sems["xw"], cnt["xw"])
            seminc(gp, "gp")
        wt2_c = cnt["gp"]

        last = {}
        um_c = {}
        stt_c = {}
        xw_c = {}
        for it in range(18):
            cc = it
            if cc < 16:
                ub = [2, 4][cc % 2]
                t2 = [T2AF, T2BF][cc % 2]

                @blk.tensor
                def _(t):
                    if cc >= 2:
                        t.wait_ge(sems["dv"], stt_c[cc - 2])
                    for tt in range(2):
                        i = t.matmul(PSF[ub + tt][:, 0:512],
                                     bfv(SF, WM2 + cc * 128, 128, p0=96, parts=32),
                                     bfv(SF, MMT + tt * 512, 512, p0=96, parts=32),
                                     start=True, stop=True, tile_position=(96, 0))
                    inc(i, "pe")
                um_c[cc] = cnt["pe"]

                @VEC
                def _(v):
                    v.wait_ge(sems["pe"], um_c[cc])
                    if cc >= 2:
                        v.wait_ge(sems["gp"], xw_c[cc - 2])
                    for tt in range(2):
                        i = v.scalar_tensor_tensor(
                            f32v(SF, t2 + tt * 512, 512),
                            PSF[ub + tt][:, 0:512], vecs["maaw_c"][:, cc:cc + 1],
                            ch(SE, cc, tt * 512, tt * 512 + 512), OP.add, OP.mult)
                        inc(i, "dv")
                stt_c[cc] = cnt["dv"]

            td = it - 2
            if 0 <= td < 16:
                @blk.tensor
                def _(t):
                    t.wait_ge(sems["gp"], xw_c[td])
                    if td == 0:
                        t.wait_ge(sems["gp"], wt2_c)
                    for tt in range(2):
                        i = t.matmul(PSF[tt][0:64, 0:512],
                                     bfv(SF, W1 + td * 64, 64),
                                     bfv(SF, [XXA, XXB][td % 2] + tt * 512, 512),
                                     start=(td == 0), stop=(td == 15))
                    inc(i, "pe")
                last[("td", td)] = cnt["pe"]

            if cc < 16:
                @GPS
                def _(gp):
                    gp.wait_ge(sems["dv"], stt_c[cc])
                    if cc >= 2:
                        gp.wait_ge(sems["pe"], last.get(("td", cc - 2), 0))
                    for tt in range(2):
                        i = gp.tensor_tensor(bfv(SF, [XXA, XXB][cc % 2] + tt * 512, 512),
                                             xn[:, cc, tt * 512:tt * 512 + 512],
                                             f32v(SF, t2 + tt * 512, 512), OP.add)
                        inc(i, "gp")
                xw_c[cc] = cnt["gp"]
        td_pe = cnt["pe"]

        @SCA
        def _(sc):
            sc.wait_ge(sems["pe"], td_pe)
            for tt in range(2):
                inc(sc.activation(bfv(SF, TDH + tt * 512, 512, parts=64),
                                  PSF[tt][0:64, 0:512], AF.Tanh), "ac")
        barrier()
        wmc, vbc, crc, f1c = {}, {}, {}, {}
        ac0 = cnt["ac"]
        ewc, bigc = {}, {}
        acc = ac0
        for it in range(19):
            c2p = it - 2
            if 0 <= c2p < 16:
                acc += 4
                bigc[c2p] = acc
            if it < 16:
                acc += 2
                ewc[it] = acc
        for it in range(19):
            cc = it
            if cc < 16:
                wb = [2, 4][cc % 2]

                @blk.tensor
                def _(t):
                    if cc >= 2:
                        t.wait_ge(sems["ac"], ewc[cc - 2])
                    for tt in range(2):
                        i = t.matmul(PSF[wb + tt][:, 0:512],
                                     bfv(SF, W0 + cc * 128, 128, parts=64),
                                     bfv(SF, TDH + tt * 512, 512, parts=64),
                                     start=True, stop=True)
                    inc(i, "pe")
                wmc[cc] = cnt["pe"]

            c2 = it - 2
            if 0 <= c2 < 16:
                base = 4096 * (c2 % 2)
                E2 = base + 3072

                @VEC
                def _(v):
                    v.wait_ge(sems["ac"], bigc[c2])
                    i = v.scalar_tensor_tensor(ch(SD, c2), f32v(SBs, E2, 1024), 1.0,
                                               f32v(SBs, base + 1024, 1024),
                                               OP.subtract, OP.mult)
                    inc(i, "dv")
                f1c[c2] = cnt["dv"]

            c1 = it - 1
            if 0 <= c1 < 16:
                base = 4096 * (c1 % 2)
                EW, GC, CR = base, base + 1024, base + 2048
                gsl = (c1 % 2) * 16

                @VEC
                def _(v):
                    v.wait_ge(sems["ac"], ewc[c1])
                    if c1 >= 2:
                        v.wait_ge(sems["gp"], crc[c1 - 2])
                    inc(v.tensor_scalar(f32v(SBs, EW, 1024), f32v(SBs, EW, 1024),
                                        5.0, -1.0, OP.min, OP.mult), "dv")
                    inc(v.tensor_tensor_scan(f32v(SBs, GC, 1024), f32v(SBs, EW, 1024),
                                             f32v(SBs, EW, 1024), 0.0, OP.add, OP.bypass), "dv")
                    gcv = f32v(SBs, GC, 1024).rearrange("p (n l) -> p n l", n=16)
                    inc(v.tensor_copy(gb0s[:, gsl:gsl + 16].unsqueeze(2),
                                      gcv[:, :, 63:64]), "dv")
                    inc(v.tensor_copy(f32v(SBs, CR, 64), f32v(SBs, GC, 64)), "dv")
                    inc(v.tensor_copy(goffT[ti][:, c1 * 16:c1 * 16 + 1],
                                      gcar[:, c1:c1 + 1]), "dv")
                    inc(v.tensor_scalar(goffT[ti][:, c1 * 16 + 1:c1 * 16 + 16],
                                        gb0s[:, gsl:gsl + 15], gcar[:, c1:c1 + 1],
                                        None, OP.add), "dv")
                    inc(v.tensor_tensor(gcar[:, c1:c1 + 1], gcar[:, c1:c1 + 1],
                                        gb0s[:, gsl + 15:gsl + 16], OP.add), "dv")
                vbc[c1] = cnt["dv"]

                @GPS
                def _(gp):
                    gp.wait_ge(sems["dv"], vbc[c1])
                    if c1 >= 2:
                        gp.wait_ge(sems["ac"], bigc[c1 - 2])
                    gcv = f32v(SBs, GC, 1024).rearrange("p (n l) -> p n l", n=16)
                    crv = f32v(SBs, CR, 1024).rearrange("p (n l) -> p n l", n=16)
                    i = gp.tensor_tensor(crv[:, 1:16, :], gcv[:, 1:16, :],
                                         gb0s[:, gsl:gsl + 15].unsqueeze(2)
                                         .broadcast_to([128, 15, 64]), OP.subtract)
                    inc(i, "gp")
                crc[c1] = cnt["gp"]

        @blk.scalar
        def _(sc):
            for it in range(19):
                c2 = it - 2
                if 0 <= c2 < 16:
                    base = 4096 * (c2 % 2)
                    EW, GC, CR, E2 = base, base + 1024, base + 2048, base + 3072
                    sc.wait_ge(sems["gp"], crc[c2])
                    if c2 >= 2:
                        sc.wait_ge(sems["dv"], f1c[c2 - 2])
                    sc.wait_ge(sems["ac"], cnt["ac"])
                    crv = f32v(SBs, CR, 1024).rearrange("p (n l) -> p n l", n=16)
                    inc(sc.activation(Gam[:, c2 * 16:(c2 + 1) * 16].unsqueeze(2),
                                      crv[:, :, 63:64], AF.Exp, scale=1.0), "ac")
                    inc(sc.activation(ch(SC, c2), f32v(SBs, CR, 1024), AF.Exp,
                                      bias=epsc[:, 1:2], scale=1.0), "ac")
                    inc(sc.activation(f32v(SBs, E2, 1024), f32v(SBs, EW, 1024),
                                      AF.Exp, scale=1.0), "ac")
                    inc(sc.activation(f32v(SBs, GC, 1024), f32v(SBs, CR, 1024),
                                      AF.Exp, scale=-1.0), "ac")
                    assert cnt["ac"] == bigc[c2], (cnt["ac"], bigc[c2])
                cc = it
                if cc < 16:
                    wb = [2, 4][cc % 2]
                    sc.wait_ge(sems["pe"], wmc[cc])
                    if cc >= 2:
                        sc.wait_ge(sems["dv"], vbc[cc - 2])
                    sc.wait_ge(sems["ac"], cnt["ac"])
                    base = 4096 * (cc % 2)
                    for tt in range(2):
                        i = sc.activation(f32v(SBs, base + tt * 512, 512),
                                          PSF[wb + tt][:, 0:512], AF.Exp,
                                          bias=vecs["tdec_c"][:, cc:cc + 1], scale=1.0)
                        inc(i, "ac")
                    assert cnt["ac"] == ewc[cc], (cnt["ac"], ewc[cc])
        barrier()

    def phase_S3(ti, kind):
        xn = v3(SA)
        r, maa, bc, Wn = {"q": (0, "maar_c", "qb_c", "Wq"),
                          "k": (1, "maak_c", "kb_c", "Wk"),
                          "v": (2, "maav_c", "vb_c", "Wv")}[kind]
        dst = {"q": SC, "k": SD, "v": SE}[kind]
        last = {}
        for cc in range(16):
            ub = [2, 4][cc % 2]
            t2 = [T2AF, T2BF][cc % 2]

            @blk.tensor
            def _(t):
                if cc >= 2:
                    t.wait_ge(sems["dv"], last.get(("stt", cc - 2), 0))
                for tt in range(2):
                    i = t.matmul(PSF[ub + tt][:, 0:512],
                                 bfv(SF, WM2 + cc * 128, 128, p0=32 * r, parts=32),
                                 bfv(SF, MMT + tt * 512, 512, p0=32 * r, parts=32),
                                 start=True, stop=True,
                                 tile_position=(32 * r, 0))
                inc(i, "pe")
            um_c = cnt["pe"]

            @VEC
            def _(v):
                v.wait_ge(sems["pe"], um_c)
                if cc >= 2:
                    v.wait_ge(sems["gp"], last.get(("xp", cc - 2), 0))
                for tt in range(2):
                    i = v.scalar_tensor_tensor(
                        f32v(SF, t2 + tt * 512, 512),
                        PSF[ub + tt][:, 0:512], vecs[maa][:, cc:cc + 1],
                        ch(SE, cc, tt * 512, tt * 512 + 512), OP.add, OP.mult)
                    inc(i, "dv")
                i = v.tensor_tensor(bfv(SBs, cc * 1024 + 512, 512),
                                    xn[:, cc, 512:1024],
                                    f32v(SF, t2 + 512, 512), OP.add)
                inc(i, "dv")
            last[("stt", cc)] = cnt["dv"]
            stt_c = cnt["dv"]

            @GPS
            def _(gp):
                gp.wait_ge(sems["dv"], stt_c)
                i = gp.tensor_tensor(bfv(SBs, cc * 1024, 512),
                                     xn[:, cc, 0:512],
                                     f32v(SF, t2, 512), OP.add)
                inc(i, "gp")
            last[("xp", cc)] = cnt["gp"]
        xp_gp = cnt["gp"]

        vAc = {}
        if kind == "v":
            vac0 = cnt["ac"]
            for cc in range(16):
                for tt in range(2):
                    vAc[(cc, tt)] = vac0 + 2 * cc + tt + 1
        vmm = {}
        for cc in range(16):
            wb = [W0, W1][cc % 2]

            @blk.sync
            def _(s):
                s.wait_ge(sems["cc"], conv_cc[0])
                if cc >= 2:
                    s.wait_ge(sems["pe"], last[("mm", cc - 2)])
                inc(s.dma_start(bfv(SF, wb, 2048), conv_slab(SLAB[(Wn, cc)])),
                    ["wa", "wb"][cc % 2], 16)
            wl_c = cnt[["wa", "wb"][cc % 2]]

            for tt in range(2):
                pb = (cc % 2) * 2 + tt
                @blk.tensor
                def _(t):
                    t.wait_ge(sems[["wa", "wb"][cc % 2]], wl_c)
                    t.wait_ge(sems["gp"], xp_gp)
                    t.wait_ge(sems["dv"], last.get(("ev", tt, cc - 2), 0))
                    if kind == "v" and cc >= 2:
                        t.wait_ge(sems["ac"], vAc[(cc - 2, tt)])
                    for kc in range(16):
                        i = t.matmul(PSF[pb][:, 0:512],
                                     bfv(SF, wb + kc * 128, 128),
                                     bfv(SBs, kc * 1024 + tt * 512, 512),
                                     start=(kc == 0), stop=(kc == 15))
                    inc(i, "pe")
                mm_c = cnt["pe"]
                vmm[(cc, tt)] = mm_c
                if tt == 1:
                    last[("mm", cc)] = mm_c

                if kind == "v":
                    pass
                else:
                    @VEC
                    def _(v):
                        v.wait_ge(sems["pe"], mm_c)
                        d = ch(dst, cc, tt * 512, tt * 512 + 512)
                        fac = ch(dst, cc, tt * 512, tt * 512 + 512)
                        i = v.scalar_tensor_tensor(d, PSF[pb][:, 0:512],
                                                   vecs[bc][:, cc:cc + 1], fac,
                                                   OP.add, OP.mult)
                        inc(i, "dv")
                    last[("ev", tt, cc)] = cnt["dv"]
        if kind == "v":
            @blk.scalar
            def _(sc):
                for cc in range(16):
                    for tt in range(2):
                        pb = (cc % 2) * 2 + tt
                        sc.wait_ge(sems["pe"], vmm[(cc, tt)])
                        d = ch(dst, cc, tt * 512, tt * 512 + 512)
                        i = sc.activation(d, PSF[pb][:, 0:512], AF.Identity,
                                          bias=vecs[bc][:, cc:cc + 1], scale=1.0)
                        inc(i, "ac")
                        assert cnt["ac"] == vAc[(cc, tt)]
        barrier()

    def phase_S4(ti):
        xn = v3(SA)
        last = {}
        ac0 = cnt["ac"]
        hAc = {0: ac0 + 1, 1: ac0 + 3}
        sgAc = {0: ac0 + 2, 1: ac0 + 4}
        gAc = {}
        for cc in range(16):
            for tt in range(2):
                gAc[(cc, tt)] = ac0 + 4 + 2 * cc + tt + 1
        pe_pend = {}
        dv_pend = {}

        @GPS
        def _(gp):
            inc(gp.dma_start(bfv(SBs, B_WR1, 1024), dram["Wr1"]), "xw", 16)
            gp.wait_ge(sems["xw"], cnt["xw"])
            seminc(gp, "gp")
        wr1_c = cnt["gp"]

        for tt in range(2):
            @blk.tensor
            def _(t):
                t.wait_ge(sems["gp"], wr1_c)
                t.wait_ge(sems["ac"], hAc[tt - 1] if tt >= 1 else 0)
                for cc in range(16):
                    i = t.matmul(PSF[2][0:64, 0:512],
                                 bfv(SBs, B_WR1 + cc * 64, 64),
                                 xn[:, cc, tt * 512:tt * 512 + 512],
                                 start=(cc == 0), stop=(cc == 15))
                inc(i, "pe")
            pe_pend[("h", tt)] = cnt["pe"]

            @blk.tensor
            def _(t):
                t.wait_ge(sems["ac"], hAc[tt])
                t.wait_ge(sems["dv"], last.get(("ltm", tt - 1), 0))
                for sl in range(4):
                    i = t.matmul(PSF[3][:, sl * 16:sl * 16 + 16],
                                 bfv(SBs, B_H + tt * 512 + sl * 128, 128, parts=64),
                                 wr2s[:], start=True, stop=False)
                    i = t.matmul(PSF[3][:, sl * 16:sl * 16 + 16], onr[:, 0:128],
                                 rb2r[:], start=False, stop=False)
                    i = t.matmul(PSF[3][:, sl * 16:sl * 16 + 16], onr[:, 0:128],
                                 gatr[:], start=False, stop=True)
                inc(i, "pe")
            lg_pe = cnt["pe"]

            @VEC
            def _(v):
                v.wait_ge(sems["pe"], lg_pe)
                ltm = f32v(SBs, LTMF, 64)
                l3 = ltm.rearrange("p (a b) -> p a b", a=4)
                inc(v.tensor_copy(ltm, PSF[3][:, 0:64]), "dv")
                m1 = f32v(SBs, M1F, 4)
                inc(v.tensor_reduce(m1.unsqueeze(2), l3, axis=AX.X, op=OP.max), "dv")
                i1 = f32v(SBs, I1F, 64).rearrange("p (a b) -> p a b", a=4)
                inc(v.tensor_tensor(i1, l3, m1.unsqueeze(2).broadcast_to([128, 4, 16]),
                                    OP.is_ge), "dv")
                lm = f32v(SBs, LMF, 64).rearrange("p (a b) -> p a b", a=4)
                inc(v.scalar_tensor_tensor(lm, f32v(SBs, I1F, 64).rearrange(
                    "p (a b) -> p a b", a=4), -1e30, l3, OP.mult, OP.add), "dv")
                m2 = f32v(SBs, M2F, 4)
                inc(v.tensor_reduce(m2.unsqueeze(2), lm, axis=AX.X, op=OP.max), "dv")
                i2 = f32v(SBs, I2F, 64).rearrange("p (a b) -> p a b", a=4)
                inc(v.tensor_tensor(i2, l3, m2.unsqueeze(2).broadcast_to([128, 4, 16]),
                                    OP.is_ge), "dv")
                inc(v.tensor_tensor(f32v(SBs, P1F, 4), m1, m2, OP.subtract), "dv")
            dv_pend[("cf", tt)] = cnt["dv"]

            @VEC
            def _(v):
                v.wait_ge(sems["ac"], sgAc[tt])
                p1 = f32v(SBs, P1F, 4)
                inc(v.tensor_scalar(f32v(SBs, PDF, 4), p1, 2.0, -1.0, OP.mult, OP.add), "dv")
                inc(v.tensor_scalar(f32v(SBs, P2F, 4), p1, -1.0, 1.0, OP.mult, OP.add), "dv")
                i2 = f32v(SBs, I2F, 64).rearrange("p (a b) -> p a b", a=4)
                i1 = f32v(SBs, I1F, 64).rearrange("p (a b) -> p a b", a=4)
                c1 = f32v(SBs, C1F, 64).rearrange("p (a b) -> p a b", a=4)
                c2 = f32v(SBs, C2F, 64).rearrange("p (a b) -> p a b", a=4)
                inc(v.tensor_tensor(c1, i2, f32v(SBs, P2F, 4).unsqueeze(2)
                                    .broadcast_to([128, 4, 16]), OP.mult), "dv")
                inc(v.tensor_tensor(c2, i1, f32v(SBs, PDF, 4).unsqueeze(2)
                                    .broadcast_to([128, 4, 16]), OP.mult), "dv")
                inc(v.tensor_tensor(bfv(SBs, B_COF, 64), f32v(SBs, C1F, 64),
                                    f32v(SBs, C2F, 64), OP.add), "dv")
            co_c = cnt["dv"]

            @blk.tensor
            def _(t):
                t.wait_ge(sems["dv"], co_c)
                for sl in range(4):
                    i = t.matmul(PSB[0][0:16, sl * 128:sl * 128 + 128],
                                 bfv(SBs, B_COF + sl * 16, 16), idn[:],
                                 start=True, stop=True, is_transpose=True)
                inc(i, "pe")
            ct_pe = cnt["pe"]

            @VEC
            def _(v):
                v.wait_ge(sems["pe"], ct_pe)
                inc(v.tensor_copy(bfv(SBs, B_CFF, 512, parts=16), PSB[0][0:16, 0:512]), "dv")
            cff_c = cnt["dv"]

            if tt == 0:
                @blk.sync
                def _(s):
                    s.wait_ge(sems["cc"], conv_cc[0])
                    inc(s.dma_start(bfv(SBs, B_WS0, 2048), conv_slab(80)), "wa", 16)
                    inc(s.dma_start(bfv(SBs, B_WS1, 2048), conv_slab(81)), "wb", 16)
            wa_c = cnt["wa"]
            wb2_c = cnt["wb"]

            @blk.tensor
            def _(t):
                t.wait_ge(sems["dv"], cff_c)
                t.wait_ge(sems["wa"], wa_c)
                t.wait_ge(sems["wb"], wb2_c)
                t.wait_ge(sems["dv"], last.get(("wxa", tt - 1), 0))
                i = t.matmul(PSF[4][:, 0:512], repR[:], bfv(SBs, B_CFF, 512, parts=16),
                             start=True, stop=True)
                inc(i, "pe")
            crep_pe = cnt["pe"]

            @VEC
            def _(v):
                v.wait_ge(sems["pe"], crep_pe)
                inc(v.tensor_copy(bfv(SBs, B_CRP, 512), PSF[4][:, 0:512]), "dv")
            crep_dv = cnt["dv"]

            @blk.tensor
            def _(t):
                for kc in range(2):
                    ws = [B_WS0, B_WS1][kc]
                    for cin in range(16):
                        i = t.matmul(PSF[kc][:, 0:512], bfv(SBs, ws + cin * 128, 128),
                                     xn[:, cin, tt * 512:tt * 512 + 512],
                                     start=(cin == 0), stop=(cin == 15))
                    inc(i, "pe")
            xa_pe = cnt["pe"]

            @VEC
            def _(v):
                v.wait_ge(sems["pe"], xa_pe)
                v.wait_ge(sems["dv"], crep_dv)
                for kc in range(2):
                    i = v.tensor_tensor(bfv(SBs, B_WXA + kc * 1024 + tt * 512, 512),
                                        PSF[kc][:, 0:512], bfv(SBs, B_CRP, 512), OP.mult)
                    inc(i, "dv")
            last[("wxa", tt)] = cnt["dv"]
            last[("ltm", tt)] = cnt["dv"]
        wxa_dv = cnt["dv"]

        g_pend = {}
        for cc in range(16):
            wb = [B_WS0, B_WS1][cc % 2]

            @blk.sync
            def _(s):
                s.wait_ge(sems["dv"], wxa_dv)
                if cc >= 2:
                    s.wait_ge(sems["ac"], gAc[(cc - 2, 1)])
                inc(s.dma_start(bfv(SBs, wb, 2048), conv_slab(SLAB[("Wf", cc)])),
                    ["wa", "wb"][cc % 2], 16)
            wf_c = cnt[["wa", "wb"][cc % 2]]

            @GPS
            def _(gp):
                gp.wait_ge(sems["dv"], wxa_dv)
                if cc >= 2:
                    gp.wait_ge(sems["ac"], gAc[(cc - 2, 1)])
                inc(gp.dma_start(bfv(SBs, [B_WB, B_WB2][cc % 2], 256), dram["Wb"][cc]),
                    ["va", "vb"][cc % 2], 16)
            vf_c = cnt[["va", "vb"][cc % 2]]

            for tt in range(2):
                pb = (cc % 2) * 2 + tt
                @blk.tensor
                def _(t):
                    t.wait_ge(sems[["wa", "wb"][cc % 2]], wf_c)
                    t.wait_ge(sems[["va", "vb"][cc % 2]], vf_c)
                    t.wait_ge(sems["ac"], gAc[(cc - 2, tt)] if cc >= 2 else 0)
                    if cc < 2:
                        t.wait_ge(sems["dv"], wxa_dv)
                    for kc in range(16):
                        t.matmul(PSF[pb][:, 0:512], bfv(SBs, wb + kc * 128, 128),
                                 xn[:, kc, tt * 512:tt * 512 + 512],
                                 start=(kc == 0), stop=False)
                    for kc in range(2):
                        i = t.matmul(PSF[pb][:, 0:512],
                                     bfv(SBs, [B_WB, B_WB2][cc % 2] + kc * 128, 128),
                                     bfv(SBs, B_WXA + kc * 1024 + tt * 512, 512),
                                     start=False, stop=(kc == 1))
                    inc(i, "pe")
                g_pend[(cc, tt)] = cnt["pe"]

        @blk.scalar
        def _(sc):
            for tt in range(2):
                sc.wait_ge(sems["pe"], pe_pend[("h", tt)])
                inc(sc.activation(bfv(SBs, B_H + tt * 512, 512, parts=64),
                                  PSF[2][0:64, 0:512], AF.Silu,
                                  bias=rb1[:, 0:1], scale=1.0), "ac")
                assert cnt["ac"] == hAc[tt]
                sc.wait_ge(sems["dv"], dv_pend[("cf", tt)])
                inc(sc.activation(f32v(SBs, P1F, 4), f32v(SBs, P1F, 4),
                                  AF.Sigmoid, scale=1.0), "ac")
                assert cnt["ac"] == sgAc[tt]
            for cc in range(16):
                for tt in range(2):
                    pb = (cc % 2) * 2 + tt
                    sc.wait_ge(sems["pe"], g_pend[(cc, tt)])
                    i = sc.activation(ch(SF, cc, tt * 512, tt * 512 + 512),
                                      PSF[pb][:, 0:512], AF.Silu,
                                      bias=vecs["fb_c"][:, cc:cc + 1], scale=1.0)
                    inc(i, "ac")
                    assert cnt["ac"] == gAc[(cc, tt)]
        barrier()

    def c3(slot, plo, pn, cc, t0, t1):
        return v3(slot)[plo:plo + pn, cc:cc + 1, t0:t1].squeeze(1)

    def phase_S5(ti):
        qt, kt, vt = SC, SD, SE
        last = {}

        @VEC
        def _(v):
            inc(v.tensor_copy(S_bf, S_all[:]), "dv")
        last[("sbf", -1)] = cnt["dv"]
        for m in range(8):
            p0 = m * 128

            @GPS
            def _(gp):
                if m >= 1:
                    gp.wait_ge(sems["pe"], last[("ktp", m - 1)])
                kv = v3(kt)[:, :, p0:p0 + 128].rearrange("p c (n l) -> p c n l", n=2)
                gv = Gam.ap().rearrange("p (c n) -> p c n", c=16)[:, :, 2 * m:2 * m + 2]
                i = gp.tensor_tensor(bfv(SBs, B_KF, 2048).rearrange(
                    "p (c n l) -> p c n l", c=16, n=2), kv,
                    gv.unsqueeze(3).broadcast_to([128, 16, 2, 64]), OP.mult)
                inc(i, "gp")
            kf_c = cnt["gp"]

            if S5CUT < 2:
                continue
            for (src_is_v, rnd) in [(True, 0), (True, 1), (False, 0), (False, 1)]:
                bb = rnd
                @blk.tensor
                def _(t):
                    if not src_is_v:
                        t.wait_ge(sems["gp"], kf_c)
                    t.wait_ge(sems["dv"], last.get(("tev", bb), 0))
                    for j in range(8):
                        hp = rnd * 8 + j
                        if src_is_v:
                            srcv = c3(vt, 0, 128, hp, p0, p0 + 128)
                        else:
                            srcv = bfv(SBs, B_KF + hp * 128, 128)
                        i = t.matmul(PSB[bb][:, j * 128:(j + 1) * 128], srcv, idn[:],
                                     start=True, stop=True, is_transpose=True)
                    inc(i, "pe")
                tp_c = cnt["pe"]
                if (not src_is_v) and rnd == 1:
                    last[("ktp", m)] = tp_c

                @VEC
                def _(v):
                    v.wait_ge(sems["pe"], tp_c)
                    dstoff = (B_VT if src_is_v else B_KT) + rnd * 1024
                    inc(v.tensor_copy(bfv(SBs, dstoff, 1024), PSB[bb][:, 0:1024]), "dv")
                last[("tev", bb)] = cnt["dv"]
            tm_dv = cnt["dv"]

            def tmoff(base, hp_, pr):
                return base + (hp_ // 8) * 1024 + (hp_ % 8) * 128 + pr * 64

            if S5CUT < 3:
                continue
            @blk.tensor
            def _(t):
                t.wait_ge(sems["dv"], last.get(("supd", 2 * m - 1), 0))
                for n in (0, 1):
                    for h in range(32):
                        hp_, pr = h // 2, h % 2
                        bank = 2 * (h % 2) + h // 16
                        slot = (h % 16) // 2
                        i = t.matmul(
                            PSF[bank][n * 64:n * 64 + 64, slot * 64:slot * 64 + 64],
                            c3(kt, pr * 64, 64, hp_, p0 + n * 64, p0 + n * 64 + 64),
                            c3(qt, pr * 64, 64, hp_, p0 + n * 64, p0 + n * 64 + 64),
                            start=True, stop=True, tile_position=(pr * 64, n * 64))
                inc(i, "pe")
            b_pe = cnt["pe"]

            if S5CUT < 4:
                continue
            @VEC
            def _(v):
                v.wait_ge(sems["pe"], b_pe)
                for bank in range(4):
                    i = v.tensor_tensor(
                        bfv(SBs, B_AM + bank * 512, 512).rearrange("p (s l) -> p s l", s=8),
                        PSF[bank].ap().rearrange("p (s l) -> p s l", s=8),
                        tri.ap().unsqueeze(1).broadcast_to([128, 8, 64]), OP.mult)
                    inc(i, "dv")
            am_dv = cnt["dv"]

            if S5CUT < 5:
                continue
            for n2 in (0, 1):
                n = 2 * m + n2
                @blk.tensor
                def _(t):
                    t.wait_ge(sems["dv"], am_dv)
                    t.wait_ge(sems["dv"], tm_dv)
                    t.wait_ge(sems["dv"], last.get(("oev", n - 1), 0))
                    t.wait_ge(sems["dv"], last.get(("sbf", n - 1), 0))
                    for h in range(32):
                        hp_, pr = h // 2, h % 2
                        bank, slot = h // 16, (h % 16) // 2
                        amv = bfv(SBs, B_AM + (2 * (h % 2) + h // 16) * 512
                                  + ((h % 16) // 2) * 64, 64, p0=n2 * 64, parts=64)
                        vtv = bfv(SBs, tmoff(B_VT, hp_, pr), 64, p0=n2 * 64, parts=64)
                        t.matmul(PSF[bank][pr * 64:pr * 64 + 64, slot * 64:slot * 64 + 64],
                                 vtv, amv, start=True, stop=True,
                                 tile_position=(n2 * 64, pr * 64))
                    for h in range(32):
                        hp_, pr = h // 2, h % 2
                        bank, slot = 2 + h // 16, (h % 16) // 2
                        qv = c3(qt, pr * 64, 64, hp_, p0 + n2 * 64, p0 + n2 * 64 + 64)
                        sv = S_bf.rearrange("p (s l) -> p s l", s=16)[
                            pr * 64:pr * 64 + 64, hp_:hp_ + 1, :].squeeze(1)
                        i = t.matmul(PSF[bank][pr * 64:pr * 64 + 64, slot * 64:slot * 64 + 64],
                                     sv, qv, start=True, stop=True,
                                     tile_position=(pr * 64, pr * 64))
                    inc(i, "pe")
                    for h in range(32):
                        hp_, pr = h // 2, h % 2
                        bank, slot = 4 + h // 16, (h % 16) // 2
                        ktv = bfv(SBs, tmoff(B_KT, hp_, pr), 64, p0=n2 * 64, parts=64)
                        i = t.matmul(PSF[bank][pr * 64:pr * 64 + 64, slot * 64:slot * 64 + 64],
                                     ktv, bfv(SBs, tmoff(B_VT, hp_, pr), 64,
                                              p0=n2 * 64, parts=64),
                                     start=True, stop=True,
                                     tile_position=(n2 * 64, pr * 64))
                    inc(i, "pe")
                ods_pe = cnt["pe"]

                @GPS
                def _(gp):
                    gp.wait_ge(sems["dv"], last.get(("supd", n - 1), 0))
                    gv = Gam.ap().rearrange("p (c n) -> p c n", c=16)[:, :, n:n + 1]
                    i = gp.tensor_tensor(
                        f32v(SBs, TS5F, 1024).rearrange("p (s l) -> p s l", s=16),
                        S_all.ap().rearrange("p (s l) -> p s l", s=16),
                        gv.broadcast_to([128, 16, 64]), OP.mult)
                    inc(i, "gp")
                ts_c = cnt["gp"]

                @VEC
                def _(v):
                    v.wait_ge(sems["pe"], ods_pe)
                    yv = v3(SA)
                    for bank in range(2):
                        i = v.tensor_scalar(
                            yv[:, bank * 8:bank * 8 + 8, n * 64:n * 64 + 64],
                            PSF[bank].ap().rearrange("p (s l) -> p s l", s=8),
                            -1.0, None, OP.mult)
                        inc(i, "dv")
                        i = v.tensor_tensor(
                            yv[:, bank * 8:bank * 8 + 8, n * 64:n * 64 + 64],
                            yv[:, bank * 8:bank * 8 + 8, n * 64:n * 64 + 64],
                            PSF[2 + bank].ap().rearrange("p (s l) -> p s l", s=8),
                            OP.subtract)
                        inc(i, "dv")
                    last[("oev", n)] = cnt["dv"]
                    v.wait_ge(sems["gp"], ts_c)
                    for bank in range(2):
                        i = v.tensor_tensor(
                            S_all[:, bank * 512:bank * 512 + 512],
                            f32v(SBs, TS5F + bank * 512, 512),
                            PSF[4 + bank][:, 0:512], OP.add)
                        inc(i, "dv")
                supd_c = cnt["dv"]
                last[("supd", n)] = supd_c

                @VEC
                def _(v):
                    inc(v.tensor_copy(S_bf, S_all[:]), "dv")
                last[("sbf", n)] = cnt["dv"]
        barrier()

    expg_ac = {}

    def phase_expg():
        ac0 = cnt["ac"]
        for j, ti in enumerate((1, 0)):
            for cc in range(16):
                expg_ac[(ti, cc)] = ac0 + j * 16 + cc + 1

        @blk.scalar
        def _(sc):
            for ti in (1, 0):
                for cc in range(16):
                    i = sc.activation(f32v(SBs, EXG16 + ti * 256 + cc * 16, 16),
                                      goffT[ti][:, cc * 16:(cc + 1) * 16], AF.Exp,
                                      scale=1.0)
                    inc(i, "ac")
                    assert cnt["ac"] == expg_ac[(ti, cc)]

    def phase_exchange():
        @GPS
        def _(gp):
            inc(gp.memset(S_inv, 0.0), "gp")
            inc(gp.dma_start(SE[:, :], sp_y), "va", 16)
        z_c = cnt["gp"]
        spy_rl = cnt["va"]

        @blk.sync
        def _(s):
            inc(s.dma_start(cc_in, S_all[:]), "xs", 16)
            s.wait_ge(sems["xs"], cnt["xs"])
            seminc(s, "sy")
        ci_c = cnt["sy"]

        @GPS
        def _(gp):
            gp.wait_ge(sems["sy"], ci_c)
            i = gp.collective_compute("AllGather", OP.bypass,
                                      replica_groups=[list(range(8))],
                                      ins=[cc_in], outs=[cc_out])
            inc(i, "cc")
        cc_c = cnt["cc"]

        last = 0
        for j, r in enumerate((0, 2, 4, 6)):
            gb = [GB0F, GB1F][j % 2]

            @blk.sync
            def _(s):
                s.wait_ge(sems["cc"], cc_c)
                s.wait_ge(sems["dv"], last)
                inc(s.dma_start(f32v(SBs, gb, 1024), cc_out[r]), ["ia", "ib"][j % 2], 16)
            g_c = cnt[["ia", "ib"][j % 2]]

            @VEC
            def _(v):
                v.wait_ge(sems[["ia", "ib"][j % 2]], g_c)
                v.wait_ge(sems["gp"], z_c)
                i = v.scalar_tensor_tensor(S_inv, f32v(SBs, gb, 1024),
                                           smsk[:, r:r + 1], S_inv, OP.mult, OP.add)
                inc(i, "dv")
            last = cnt["dv"]

        @VEC
        def _(v):
            inc(v.tensor_copy(bfv(SBs, B_SIB, 1024), S_inv), "dv")
        return spy_rl

    def phase_corr(ti, yslot, qslot, rl_in=None, rl_out=None, ysub=None):
        qg_c = {}
        rl_pend = {}
        for cc in range(16):
            @GPS
            def _(gp):
                gp.wait_ge(sems["ac"], expg_ac[(ti, cc)])
                gp.wait_ge(sems["pe"], cnt["pe"])
                if rl_in is not None:
                    sm, val = rl_in[cc]
                    gp.wait_ge(sems[sm], val)
                qv = v3(qslot)[:, cc:cc + 1, :].squeeze(1).rearrange("p (n l) -> p n l", n=16)
                i = gp.tensor_tensor(bfv(SBs, B_QG, 1024).rearrange("p (n l) -> p n l", n=16),
                                     qv, f32v(SBs, EXG16 + ti * 256 + cc * 16, 16)
                                     .unsqueeze(2).broadcast_to([128, 16, 64]), OP.mult)
                inc(i, "gp")
            qg_c[cc] = cnt["gp"]

            if rl_out is not None:
                src, dstslot, rsems = rl_out
                par = rsems[cc % 2]

                @blk.sync
                def _(s):
                    s.wait_ge(sems["gp"], qg_c[cc])
                    if cc >= 2:
                        s.wait_ge(sems[par], rl_pend[cc - 2])
                    inc(s.dma_start(bfv(dstslot, cc * 1024, 1024),
                                    src[:, cc * 1024:(cc + 1) * 1024]), par, 16)
                rl_pend[cc] = cnt[par]

            @blk.tensor
            def _(t):
                t.wait_ge(sems["gp"], qg_c[cc])
                t.wait_ge(sems["dv"], cnt["dv"])
                for tt in range(2):
                    for h2 in (0, 1):
                        i = t.matmul(PSF[tt][h2 * 64:h2 * 64 + 64, 0:512],
                                     bfv(SBs, B_SIB + cc * 64, 64, p0=h2 * 64, parts=64),
                                     bfv(SBs, B_QG + tt * 512, 512, p0=h2 * 64, parts=64),
                                     start=True, stop=True,
                                     tile_position=(h2 * 64, h2 * 64))
                inc(i, "pe")
            qm_c = cnt["pe"]

            @VEC
            def _(v):
                v.wait_ge(sems["pe"], qm_c)
                if cc == 0 and ysub is not None:
                    v.wait_ge(sems[ysub[0]], ysub[1])
                for tt in range(2):
                    yv = ch(yslot, cc, tt * 512, tt * 512 + 512)
                    inc(v.tensor_tensor(yv, yv, PSF[tt][:, 0:512], OP.subtract), "dv")
        return rl_pend, cnt["pe"], cnt["dv"]

    def phase_S6(ti, yslot, gslot, first, corr_pe=0, corr_dv=0, g_rl=None,
                 prev_ow=0, prev_store=(0, 0), prev_fin=0, prev_st_pe=0):
        last = {}
        if first:
            @GPS
            def _(gp):
                gp.wait_ge(sems["pe"], corr_pe)
                inc(gp.dma_start(bfv(SBs, B_LNGR, 2048, parts=1), dram["lng_r"]), "xw", 16)
                i = gp.tensor_copy(glbt.ap().rearrange("p (c two) -> p c two", two=2)[:, :, 0:1],
                                   vecs["lng_c"].ap().unsqueeze(2))
                inc(i, "gp")
                i = gp.tensor_copy(glbt.ap().rearrange("p (c two) -> p c two", two=2)[:, :, 1:2],
                                   vecs["lnb_c"].ap().unsqueeze(2))
                inc(i, "gp")
            lng_xw = cnt["xw"]
            glbt_gp = cnt["gp"]
        else:
            lng_xw = glbt_gp = 0

        ac0 = cnt["ac"]
        ysqAc = {}
        sqAc = {}
        for tt in range(2):
            for cc in range(16):
                ysqAc[(tt, cc)] = ac0 + tt * 17 + cc + 1
            sqAc[tt] = ac0 + tt * 17 + 17
        st_pend = {}
        va_pend = {}

        for tt in range(2):
            for cc in range(16):
                @blk.tensor
                def _(t):
                    t.wait_ge(sems["ac"], ysqAc[(tt, cc)])
                    if cc == 0:
                        t.wait_ge(sems["dv"], last.get(("stev", tt - 1), 0))
                        if tt == 0:
                            t.wait_ge(sems["dv"], corr_dv)
                    t.matmul(PSF[4][0:1, 0:512], onc[:],
                             ch(yslot, cc, tt * 512, tt * 512 + 512),
                             start=(cc == 0), stop=(cc == 15))
                    i = t.matmul(PSF[5][0:1, 0:512], onc[:],
                                 bfv(SBs, [B_YSQ, B_YSQ2][cc % 2], 512),
                                 start=(cc == 0), stop=(cc == 15))
                    inc(i, "pe")
                st_pend[(tt, cc)] = cnt["pe"]
            st_pe = cnt["pe"]

            @VEC
            def _(v):
                v.wait_ge(sems["pe"], st_pe)
                mu = f32v(SBs, MUF, 512, parts=1)
                va = f32v(SBs, VAF, 512, parts=1)
                rs = f32v(SBs, RSF, 512, parts=1)
                inc(v.tensor_scalar(mu, PSF[4][0:1, 0:512], 1.0 / 2048, None, OP.mult), "dv")
                inc(v.tensor_scalar(va, PSF[5][0:1, 0:512], 1.0 / 2048, None, OP.mult), "dv")
                inc(v.tensor_tensor(rs, mu, mu, OP.mult), "dv")
                inc(v.tensor_tensor(va, va, rs, OP.subtract), "dv")
            va_pend[tt] = cnt["dv"]

            @VEC
            def _(v):
                v.wait_ge(sems["ac"], sqAc[tt])
                v.wait_ge(sems["pe"], prev_ow)
                rs = f32v(SBs, RSF, 512, parts=1)
                nm = f32v(SBs, NMF, 512, parts=1)
                rh = bfv(SBs, RHB + tt * 512, 512, parts=1)
                rl = bfv(SBs, RLB + tt * 512, 512, parts=1)
                nh = bfv(SBs, NHB + tt * 512, 512, parts=1)
                nl = bfv(SBs, NLB + tt * 512, 512, parts=1)
                inc(v.reciprocal(rs, rs), "dv")
                inc(v.tensor_copy(rh, rs), "dv")
                inc(v.tensor_tensor(rl, rs, rh, OP.subtract), "dv")
                inc(v.tensor_tensor(nm, f32v(SBs, MUF, 512, parts=1), rs, OP.mult), "dv")
                inc(v.tensor_scalar(nm, nm, -1.0, None, OP.mult), "dv")
                inc(v.tensor_copy(nh, nm), "dv")
                inc(v.tensor_tensor(nl, nm, nh, OP.subtract), "dv")
            tst_c = cnt["dv"]
            last[("stev", tt)] = tst_c

            for cc in range(16):
                b1 = 2 + (cc % 2)
                @blk.tensor
                def _(t):
                    t.wait_ge(sems["dv"], tst_c)
                    t.wait_ge(sems["dv"], last.get(("yh", tt, cc - 2), 0))
                    if tt == 0 and cc == 0:
                        t.wait_ge(sems["xw"], lng_xw)
                    t.matmul(PSF[b1][:, 0:512], bfv(SBs, B_LNGR + cc * 128, 128, parts=1),
                             bfv(SBs, RHB + tt * 512, 512, parts=1), start=True, stop=False)
                    i = t.matmul(PSF[b1][:, 0:512], bfv(SBs, B_LNGR + cc * 128, 128, parts=1),
                                 bfv(SBs, RLB + tt * 512, 512, parts=1), start=False, stop=True)
                    inc(i, "pe")
                b1_c = cnt["pe"]

                @VEC
                def _(v):
                    v.wait_ge(sems["pe"], b1_c)
                    yv = ch(yslot, cc, tt * 512, tt * 512 + 512)
                    inc(v.tensor_tensor(yv, yv, PSF[b1][:, 0:512], OP.mult), "dv")
                last[("yh", tt, cc)] = cnt["dv"]
        yh_dv = cnt["dv"]

        @blk.scalar
        def _(sc):
            sc.wait_ge(sems["dv"], corr_dv)
            sc.wait_ge(sems["pe"], prev_st_pe)
            for tt in range(2):
                for cc in range(16):
                    if cc >= 2:
                        sc.wait_ge(sems["pe"], st_pend[(tt, cc - 2)])
                    elif tt == 1 and cc < 2:
                        sc.wait_ge(sems["pe"], st_pend[(0, 14 + cc)])
                    sc.wait_ge(sems["ac"], cnt["ac"])
                    i = sc.activation(bfv(SBs, [B_YSQ, B_YSQ2][cc % 2], 512),
                                      ch(yslot, cc, tt * 512, tt * 512 + 512), AF.Square)
                    inc(i, "ac")
                    assert cnt["ac"] == ysqAc[(tt, cc)]
                sc.wait_ge(sems["dv"], va_pend[tt])
                i = sc.activation(f32v(SBs, RSF, 512, parts=1), f32v(SBs, VAF, 512, parts=1),
                                  AF.Sqrt, bias=epsc[0:1, 0:1], scale=1.0)
                inc(i, "ac")
                assert cnt["ac"] == sqAc[tt]

        for cc in range(16):
            wb = [D_W0, D_W1][cc % 2]

            @blk.sync
            def _(s):
                s.wait_ge(sems["dv"], yh_dv)
                if cc >= 2:
                    s.wait_ge(sems["pe"], last.get(("ow", cc - 2), 0))
                else:
                    s.wait_ge(sems["pe"], prev_ow)
                inc(s.dma_start(bfv(SD, wb, 2048), conv_slab(SLAB[("Wo", cc)])),
                    ["wa", "wb"][cc % 2], 16)
            wo_c = cnt[["wa", "wb"][cc % 2]]

            if first:
                @blk.tensor
                def _(t):
                    t.wait_ge(sems[["wa", "wb"][cc % 2]], wo_c)
                    t.wait_ge(sems["dv"], last.get(("gbe", cc - 1), 0))
                    if cc == 0:
                        t.wait_ge(sems["gp"], glbt_gp)
                    for kc in range(16):
                        t.matmul(PSF[5][0:1, 0:128],
                                 glbt[:, 2 * kc:2 * kc + 1],
                                 bfv(SD, wb + kc * 128, 128),
                                 start=(kc == 0), stop=(kc == 15))
                    for kc in range(16):
                        i = t.matmul(PSF[5][0:1, 128:256],
                                     glbt[:, 2 * kc + 1:2 * kc + 2],
                                     bfv(SD, wb + kc * 128, 128),
                                     start=(kc == 0), stop=(kc == 15))
                    inc(i, "pe")
                gb_pe = cnt["pe"]

                @VEC
                def _(v):
                    v.wait_ge(sems["pe"], gb_pe)
                    inc(v.tensor_copy(bfv(SBs, B_GROW + cc * 128, 128, parts=1),
                                      PSF[5][0:1, 0:128]), "dv")
                    inc(v.tensor_copy(bfv(SBs, B_BROW + cc * 128, 128, parts=1),
                                      PSF[5][0:1, 128:256]), "dv")
                last[("gbe", cc)] = cnt["dv"]
            gbe_c = last.get(("gbe", cc), 0)

            for tt in range(2):
                pb = (cc % 2) * 2 + tt
                @blk.tensor
                def _(t):
                    t.wait_ge(sems[["wa", "wb"][cc % 2]], wo_c)
                    t.wait_ge(sems["dv"], gbe_c)
                    t.wait_ge(sems["dv"], last.get(("sg", tt, cc - 2), 0))
                    for kc in range(16):
                        t.matmul(PSF[pb][:, 0:512], bfv(SD, wb + kc * 128, 128),
                                 ch(yslot, kc, tt * 512, tt * 512 + 512),
                                 start=(kc == 0), stop=False)
                    t.matmul(PSF[pb][:, 0:512], bfv(SBs, B_GROW + cc * 128, 128, parts=1),
                             bfv(SBs, NHB + tt * 512, 512, parts=1), start=False, stop=False)
                    t.matmul(PSF[pb][:, 0:512], bfv(SBs, B_GROW + cc * 128, 128, parts=1),
                             bfv(SBs, NLB + tt * 512, 512, parts=1), start=False, stop=False)
                    i = t.matmul(PSF[pb][:, 0:512], bfv(SBs, B_BROW + cc * 128, 128, parts=1),
                                 onr[:], start=False, stop=True)
                    inc(i, "pe")
                s_pe = cnt["pe"]
                if tt == 1:
                    last[("ow", cc)] = s_pe

                @VEC
                def _(v):
                    v.wait_ge(sems["pe"], s_pe)
                    if g_rl is not None and tt == 0:
                        sm, val = g_rl[cc]
                        v.wait_ge(sems[sm], val)
                    gv = ch(gslot, cc, tt * 512, tt * 512 + 512)
                    inc(v.tensor_tensor(gv, PSF[pb][:, 0:512], gv, OP.mult), "dv")
                last[("sg", tt, cc)] = cnt["dv"]
        sg_dv = cnt["dv"]

        prev = {("store", 0): prev_store[0], ("store", 1): prev_store[1]}
        for sl in range(NSLAB):
            row0 = ti * TT + sl * 128
            iob = [D_IO0F, D_IO1F][sl % 2]
            xio = f32v(SD, iob, 2048)

            lsem = ["ia", "ib"][sl % 2]
            ssem = ["sa", "sb"][sl % 2]

            @blk.sync
            def _(s):
                s.wait_ge(sems[ssem], prev.get(("store", sl % 2), 0))
                inc(s.dma_start(xio, dram["xin"][row0:row0 + 128, :]), lsem, 16)
            xl_c = cnt[lsem]

            for half in range(2):
                bb = half
                @blk.tensor
                def _(t):
                    t.wait_ge(sems["dv"], sg_dv)
                    t.wait_ge(sems["dv"], prev.get(("fin", sl - 1, bb), prev_fin))
                    for j in range(8):
                        cc = half * 8 + j
                        i = t.matmul(PSB[bb][:, j * 128:(j + 1) * 128],
                                     ch(gslot, cc, sl * 128, sl * 128 + 128), idn[:],
                                     start=True, stop=True, is_transpose=True)
                    inc(i, "pe")
                tp_c = cnt["pe"]

                @VEC
                def _(v):
                    v.wait_ge(sems["pe"], tp_c)
                    v.wait_ge(sems[lsem], xl_c)
                    i = v.tensor_tensor(xio[:, half * 1024:half * 1024 + 1024],
                                        xio[:, half * 1024:half * 1024 + 1024],
                                        PSB[bb][:, 0:1024], OP.add)
                    inc(i, "dv")
                prev[("fin", sl, bb)] = cnt["dv"]
            fin_c = cnt["dv"]

            @blk.sync
            def _(s):
                s.wait_ge(sems["dv"], fin_c)
                inc(s.dma_start(out[row0:row0 + 128, :], xio), ssem, 16)
            prev[("store", sl % 2)] = cnt[ssem]
        if not first:
            barrier()
        return (last.get(("ow", 15), 0),
                (prev[("store", 0)], prev[("store", 1)]),
                prev.get(("fin", NSLAB - 1, 1), 0),
                st_pend.get((1, 15), 0))

    SLAB = {}
    for si, nm in enumerate(["Wq", "Wk", "Wv", "Wo", "Wf"]):
        for cc in range(16):
            SLAB[(nm, cc)] = si * 16 + cc
    SLAB[("Wa", 0)] = 80
    SLAB[("Wa", 1)] = 81

    def conv_slab(j):
        return conv_out[j // 11, j % 11]

    conv_cc = [0]

    def phase_conv():
        lastout = {}
        for i in range(11):
            stg = bfv(SC, (i % 2) * 2048, 2048)
            vsem = ["va", "vb"][i % 2]
            ssem = ["sa", "sb"][i % 2]

            @GPS
            def _(gp):
                if i >= 2:
                    gp.wait_ge(sems[ssem], lastout[i - 2])
                inc(gp.dma_start(stg, dram["Wsrc"][i]), vsem, 16)
            in_c = cnt[vsem]

            @blk.sync
            def _(s):
                s.wait_ge(sems[vsem], in_c)
                inc(s.dma_start(conv_in[i], stg), ssem, 16)
            lastout[i] = cnt[ssem]

        @GPS
        def _(gp):
            gp.wait_ge(sems["sa"], cnt["sa"])
            gp.wait_ge(sems["sb"], cnt["sb"])
            i = gp.collective_compute("AllGather", OP.bypass,
                                      replica_groups=[list(range(8))],
                                      ins=[conv_in], outs=[conv_out])
            inc(i, "cc")
        conv_cc[0] = cnt["cc"]

    step_n = [0]

    def step():
        step_n[0] += 1
        return step_n[0] <= LIMIT

    for ti in range(2):
        if step():
            phase_S1(ti)
        if ti == 0:
            phase_conv()
        dump("d_xn%d" % ti, SA[:, :])
        if step():
            phase_S2(ti)
        dump("d_expc%d" % ti, SC[:, :])
        dump("d_f1%d" % ti, SD[:, :])
        dump("d_gam%d" % ti, Gam[:, :])
        dump("d_goff%d" % ti, goffT[ti][:, :])
        if step():
            phase_S3(ti, "q")
        dump("d_q%d" % ti, SC[:, :])
        if step():
            phase_S3(ti, "k")
        dump("d_k%d" % ti, SD[:, :])
        if step():
            phase_S3(ti, "v")
        dump("d_v%d" % ti, SE[:, :])
        if step():
            phase_S4(ti)
        dump("d_g%d" % ti, SF[:, :])
        if ti == 0:
            @GPS
            def _(gp):
                inc(gp.tensor_copy(xnb[:], v3(SA)[:, :, 1023:1024].squeeze(2)), "gp")
            barrier()
            @blk.sync
            def _(s):
                inc(s.dma_start(sp_q, SC[:, :]), "xs", 16)
                inc(s.dma_start(sp_g, SF[:, :]), "xs", 16)
        if step():
            phase_S5(ti)
        dump("d_yr%d" % ti, SA[:, :])
        if ti == 0:
            @blk.sync
            def _(s):
                inc(s.dma_start(sp_y, SA[:, :]), "xs", 16)
            barrier()
    if step():
        phase_expg()
    spy_rl = 0
    if step():
        spy_rl = phase_exchange()
    c1 = c0 = None
    if step():
        c1 = phase_corr(1, SA, SC, rl_out=(sp_q, SC, ("ia", "ib")))
    if step():
        rl_in0 = {cc: (("ia", "ib")[cc % 2], c1[0][cc]) for cc in range(16)}
        c0 = phase_corr(0, SE, SC, rl_in=rl_in0,
                        rl_out=(sp_g, SC, ("sa", "sb")), ysub=("va", spy_rl))
    r6 = None
    if step():
        g_rl0 = {cc: (("sa", "sb")[cc % 2], c0[0][cc]) for cc in range(16)}
        r6 = phase_S6(0, SE, SC, first=True, corr_pe=c0[1], corr_dv=c0[2], g_rl=g_rl0)
    if step():
        phase_S6(1, SA, SF, first=False, prev_ow=r6[0], prev_store=r6[1],
                 prev_fin=r6[2], prev_st_pe=r6[3])

    @blk.sync
    def _(s):
        s.wait_ge(sems["sa"], cnt["sa"])
        s.wait_ge(sems["sb"], cnt["sb"])
        s.wait_ge(sems["st"], cnt["st"])

    blkcm.__exit__(None, None, None)
    for cm in reversed(ctxs):
        cm.__exit__(None, None, None)
    for cm in reversed(semcms):
        cm.__exit__(None, None, None)
    return nc


_cache = {}


def kernel(**ins):
    maps = build_inputs(ins)
    key = "prog"
    if key not in _cache:
        _cache[key] = build_program(debug=DEBUG)
    nc = _cache[key]
    res = run_bass_kernel_spmd(nc, maps, core_ids=list(range(8)))
    outp = np.empty((4, 4096, 2048), dtype=np.float32)
    for c in range(8):
        b, hf = c // 2, c % 2
        outp[b, hf * TC:(hf + 1) * TC] = res.results[c]["out"]
    return outp
